# version 03
# baseline (speedup 1.0000x reference)
import contextlib
import numpy as np
import concourse.bass as bass
import concourse.mybir as mybir
from concourse.bass_utils import run_bass_kernel_spmd

F32 = mybir.dt.float32
BF16 = mybir.dt.bfloat16
I32 = mybir.dt.int32
AF = mybir.ActivationFunctionType
ALU = mybir.AluOpType

NCORES = 8
S_FULL = 8192
TOK = S_FULL // NCORES
D = 4096
DCH = D // 128
EPS_LN = 1e-5


def bankslice(pool, bank, rows=128, row0=0, cols=512):
    return pool[row0:row0 + rows, 512 * bank:512 * bank + cols]


def load_chunked(eng, sbuf_flat, dram_ap, nch, width):
    return eng.dma_start(
        out=sbuf_flat.rearrange("p (c w) -> p c w", c=nch),
        in_=dram_ap.rearrange("(c p) w -> p c w", p=128))



def build_l0a():
    nc = bass.Bass("TRN2", target_bir_lowering=False, debug=False)
    xT = nc.dram_tensor("xT", [D, TOK], F32, kind="ExternalInput").ap()
    wq = nc.dram_tensor("wq", [D, 12], F32, kind="ExternalInput").ap()
    bq = nc.dram_tensor("bq", [12, 1], F32, kind="ExternalInput").ap()
    wkv = nc.dram_tensor("wkv", [D, 4], F32, kind="ExternalInput").ap()
    bkv = nc.dram_tensor("bkv", [4, 1], F32, kind="ExternalInput").ap()
    xn_out = nc.dram_tensor("xn", [D, TOK], BF16, kind="ExternalOutput").ap()
    cq_out = nc.dram_tensor("cqT", [12, TOK], F32, kind="ExternalOutput").ap()
    ckv_out = nc.dram_tensor("ckvT", [4, TOK], F32, kind="ExternalOutput").ap()
    es = contextlib.ExitStack()
    sb = lambda name, shape, dt=F32: es.enter_context(nc.sbuf_tensor(name, shape, dt))
    psum = lambda name, shape: es.enter_context(nc.psum_tensor(name, shape, F32))
    sem = lambda name: es.enter_context(nc.semaphore(name))
    if True:
        consts = sb("consts", [128, 4])
        sq = sb("sq", [128, 2 * TOK])
        ones1 = sb("ones1", [1, 128], BF16)
        ones1f = sb("ones1f", [1, 128])
        onescf = sb("onescf", [128, 1])
        stats = sb("stats", [1, 6 * TOK])
        Rsb = sb("Rsb", [128, TOK])
        MRsb = sb("MRsb", [128, TOK])
        wqs = sb("wqs", [128, DCH * 12], BF16)
        wkvs = sb("wkvs", [128, DCH * 4], BF16)
        bqs = sb("bqs", [12, 1])
        bkvs = sb("bkvs", [4, 1])
        xnb = sb("xnb", [128, 2 * TOK], BF16)
        xfc = sb("xfc", [128, 2 * TOK])
        cqsb = sb("cqsb", [12, TOK])
        ckvsb = sb("ckvsb", [4, TOK])
        pool = psum("pool", [128, 4096])
        ld = sem("ld"); sqd = sem("sqd"); mmst = sem("mmst"); vst = sem("vst")
        XF = [sem("xf0"), sem("xf1")]
        scd = sem("scd"); bcd = sem("bcd"); xnd = sem("xnd"); prj = sem("prj"); sto = sem("sto")
        block = es.enter_context(nc.Block())
        nc.const_aps.register(F32, float(EPS_LN), consts[:, 0:1])
        pm = pool[0:1, 0:1024]
        pv = pool[0:1, 1024:2048]
        pbc = pool[:, 2048:3072]
        pcq = pool[0:12, 3072:4096]
        pckv = pool[0:4, 0:1024]

        @block.gpsimd
        def _(g):
            g.memset(ones1[:], 1.0)
            g.memset(ones1f[:], 1.0)
            g.memset(onescf[:], 1.0)
            g.memset(consts[:, 0:1], EPS_LN)
            load_chunked(g, wqs[:], wq, DCH, 12).then_inc(ld, 16)
            load_chunked(g, wkvs[:], wkv, DCH, 4).then_inc(ld, 16)
            g.dma_start(out=bqs[:], in_=bq).then_inc(ld, 16)
            g.dma_start(out=bkvs[:], in_=bkv).then_inc(ld, 16)
            g.engine_nop().then_inc(ld, 0) if False else None
            for c in range(DCH):
                if c >= 2:
                    g.wait_ge(mmst, c - 1)
                g.dma_start(out=xfc[:, (c % 2) * TOK:(c % 2 + 1) * TOK],
                            in_=xT[c * 128:(c + 1) * 128, :]).then_inc(XF[c % 2], 16)
            for c in range(DCH):
                if c >= 2:
                    g.wait_ge(xnd, c - 1)
                else:
                    g.wait_ge(mmst, DCH)
                g.dma_start(out=xfc[:, (c % 2) * TOK:(c % 2 + 1) * TOK],
                            in_=xT[c * 128:(c + 1) * 128, :]).then_inc(XF[c % 2], 16)

        @block.scalar
        def _(sc):
            for c in range(DCH):
                sc.wait_ge(XF[c % 2], 16 * (c // 2 + 1))
                if c >= 2:
                    sc.wait_ge(mmst, c - 1)
                sc.activation(sq[:, (c % 2) * TOK:(c % 2 + 1) * TOK],
                              xfc[:, (c % 2) * TOK:(c % 2 + 1) * TOK], AF.Square).then_inc(sqd, 1)
            sc.wait_ge(vst, 1)
            sc.activation(stats[:, 5 * TOK:6 * TOK], stats[:, 4 * TOK:5 * TOK], AF.Sqrt).then_inc(scd, 1)

        @block.tensor
        def _(t):
            t.wait_ge(ld, 64)
            for c in range(DCH):
                b = (c % 2) * TOK
                xc = xfc[:, b:b + TOK]
                sqc = sq[:, b:b + TOK]
                st, sp = (c == 0), (c == DCH - 1)
                t.wait_ge(XF[c % 2], 16 * (c // 2 + 1))
                t.matmul(pm[:, 0:512], onescf[:], xc[:, 0:512], start=st, stop=sp)
                t.matmul(pm[:, 512:1024], onescf[:], xc[:, 512:1024], start=st, stop=sp)
                t.wait_ge(sqd, c + 1)
                t.matmul(pv[:, 0:512], onescf[:], sqc[:, 0:512], start=st, stop=sp)
                t.matmul(pv[:, 512:1024], onescf[:], sqc[:, 512:1024], start=st, stop=sp).then_inc(mmst, 1)
            t.wait_ge(vst, 3)
            t.matmul(pbc[:, 0:512], ones1f[:], stats[0:1, 2 * TOK:2 * TOK + 512], start=True, stop=True)
            t.matmul(pbc[:, 512:1024], ones1f[:], stats[0:1, 2 * TOK + 512:3 * TOK], start=True, stop=True).then_inc(bcd, 1)
            t.wait_ge(vst, 4)
            t.matmul(pbc[:, 0:512], ones1f[:], stats[0:1, 3 * TOK:3 * TOK + 512], start=True, stop=True)
            t.matmul(pbc[:, 512:1024], ones1f[:], stats[0:1, 3 * TOK + 512:4 * TOK], start=True, stop=True).then_inc(bcd, 2)
            for c in range(DCH):
                t.wait_ge(xnd, c + 1)
                xc = xnb[:, (c % 2) * TOK:(c % 2 + 1) * TOK]
                wqc = wqs[:, c * 12:(c + 1) * 12]
                wkvc = wkvs[:, c * 4:(c + 1) * 4]
                st, sp = (c == 0), (c == DCH - 1)
                t.matmul(pcq[:, 0:512], wqc, xc[:, 0:512], start=st, stop=sp)
                t.matmul(pcq[:, 512:1024], wqc, xc[:, 512:1024], start=st, stop=sp)
                t.matmul(pckv[:, 0:512], wkvc, xc[:, 0:512], start=st, stop=sp)
                t.matmul(pckv[:, 512:1024], wkvc, xc[:, 512:1024], start=st, stop=sp).then_inc(prj, 1)

        @block.vector
        def _(v):
            mean = stats[0:1, 0:TOK]
            var = stats[0:1, TOK:2 * TOK]
            rstd = stats[0:1, 2 * TOK:3 * TOK]
            mr = stats[0:1, 3 * TOK:4 * TOK]
            scr = stats[0:1, 4 * TOK:5 * TOK]
            srt = stats[0:1, 5 * TOK:6 * TOK]
            v.wait_ge(mmst, DCH)
            v.tensor_scalar(mean, pm, 1.0 / D, None, ALU.mult)
            v.tensor_scalar(var, pv, 1.0 / D, None, ALU.mult)
            v.tensor_tensor(scr, mean, mean, ALU.mult)
            v.tensor_tensor(var, var, scr, ALU.subtract)
            v.tensor_scalar(scr, var, float(EPS_LN), None, ALU.add).then_inc(vst, 1)
            v.wait_ge(scd, 1)
            v.reciprocal(rstd, srt).then_inc(vst, 1)
            v.wait_ge(vst, 2)
            v.tensor_tensor(mr, mean, rstd, ALU.mult).then_inc(vst, 1)
            v.wait_ge(bcd, 1)
            v.tensor_copy(Rsb[:], pbc).then_inc(vst, 1)
            v.wait_ge(bcd, 3)
            v.tensor_copy(MRsb[:], pbc)
            for c in range(DCH):
                buf = xnb[:, (c % 2) * TOK:(c % 2 + 1) * TOK]
                if c >= 2:
                    v.wait_ge(prj, c - 1)
                    v.wait_ge(sto, 16 * (c - 1))
                v.wait_ge(XF[c % 2], 16 * ((DCH + c) // 2 + 1))
                xc = xfc[:, (c % 2) * TOK:(c % 2 + 1) * TOK]
                sqx = sq[:, (c % 2) * TOK:(c % 2 + 1) * TOK]
                v.tensor_tensor(sqx, xc, Rsb[:], ALU.mult)
                v.tensor_tensor(buf, sqx, MRsb[:], ALU.subtract).then_inc(xnd, 1)
            v.wait_ge(prj, DCH)
            v.tensor_scalar(cqsb[:], pcq, bqs[:], None, ALU.add)
            v.tensor_scalar(ckvsb[:], pckv, bkvs[:], None, ALU.add).then_inc(vst, 1)

        @block.sync
        def _(sy):
            for c in range(DCH):
                sy.wait_ge(xnd, c + 1)
                sy.dma_start(out=xn_out[c * 128:(c + 1) * 128, :],
                             in_=xnb[:, (c % 2) * TOK:(c % 2 + 1) * TOK]).then_inc(sto, 16)
            sy.wait_ge(vst, 5)
            sy.dma_start(out=cq_out, in_=cqsb[:]).then_inc(sto, 16)
            sy.dma_start(out=ckv_out, in_=ckvsb[:]).then_inc(sto, 16)
            sy.wait_ge(sto, 16 * (DCH + 2))
    es.close()
    return nc


def stage_l0a_inputs(x, wq, bq, wkv, bkv):
    xT = np.ascontiguousarray(x.reshape(S_FULL, D).T)
    maps = []
    for c in range(NCORES):
        maps.append({
            "xT": np.ascontiguousarray(xT[:, c * TOK:(c + 1) * TOK]),
            "wq": wq, "bq": bq.reshape(12, 1), "wkv": wkv, "bkv": bkv.reshape(4, 1),
        })
    return maps



NKC = 64
NH = 32


def build_l0b():
    nc = bass.Bass("TRN2", target_bir_lowering=False, debug=False)
    cqT = nc.dram_tensor("cqT", [12, TOK], F32, kind="ExternalInput").ap()
    wqk = nc.dram_tensor("wqk", [12, 128], F32, kind="ExternalInput").ap()
    bqk = nc.dram_tensor("bqk", [128, 1], F32, kind="ExternalInput").ap()
    ckvT = nc.dram_tensor("ckvT", [4, S_FULL], F32, kind="ExternalInput").ap()
    c1 = nc.dram_tensor("c1", [128, NKC * 36], F32, kind="ExternalInput").ap()
    wvu = nc.dram_tensor("wvu", [4, D], F32, kind="ExternalInput").ap()
    bvu = nc.dram_tensor("bvu", [128, DCH], F32, kind="ExternalInput").ap()
    wo = nc.dram_tensor("wo", [D, D], F32, kind="ExternalInput").ap()
    bo = nc.dram_tensor("bo", [128, DCH], F32, kind="ExternalInput").ap()
    bd = nc.dram_tensor("bd", [128, NH], F32, kind="ExternalInput").ap()
    aout = nc.dram_tensor("aout", [D, TOK], F32, kind="ExternalOutput").ap()
    scr32 = nc.dram_tensor("scr32", [32], F32, kind="Internal").ap()

    es = contextlib.ExitStack()
    sb = lambda name, shape, dt=F32: es.enter_context(nc.sbuf_tensor(name, shape, dt))
    sem = lambda name: es.enter_context(nc.semaphore(name))
    if True:
        cq_s = sb("cq_s", [12, TOK], BF16)
        wqk_s = sb("wqk_s", [12, 128], BF16)
        bqk_s = sb("bqk_s", [128, 1])
        ckv_s = sb("ckv_s", [4, S_FULL], BF16)
        ckvT5 = sb("ckvT5", [5, S_FULL], BF16)
        c1_s = sb("c1_s", [128, NKC * 36], BF16)
        wvu_s = sb("wvu_s", [4, D], BF16)
        bvu_s = sb("bvu_s", [128, DCH])
        bo_s = sb("bo_s", [128, DCH])
        ones_f = sb("ones_f", [1, 128])
        onescf = sb("onescf", [128, 1])
        onescb = sb("onescb", [128, 1], BF16)
        blkdg = sb("blkdg", [128, NH], BF16)
        qall = sb("qall", [128, TOK], BF16)
        qsq = sb("qsq", [128, TOK], BF16)
        msqcol = sb("msqcol", [32, 1])
        dr32 = sb("dr32", [1, 32])
        sstat = sb("sstat", [1, 16])
        onesr_f = sb("onesr_f", [1, TOK])
        mrow = sb("mrow", [1, TOK], BF16)
        rhsb = sb("rhsb", [5, 2 * TOK], BF16)
        Eb = sb("Eb", [128, 2 * TOK], BF16)
        zi = sb("zi", [1, TOK])
        zib = sb("zib", [4, TOK])
        rhsv = sb("rhsv", [4, TOK], BF16)
        opre = sb("opre", [128, DCH * TOK], BF16)
        wos = sb("wos", [128, 2 * 16 * 512], BF16)
        oev = sb("oev", [128, 2 * TOK])
        pool = es.enter_context(nc.psum_tensor("pool", [128, 4096], F32))
        TS_, VS_, AS_, GD_, GP_, SD_ = (sem("ts"), sem("vs"), sem("as"),
                                        sem("gd"), sem("gp"), sem("sd"))
        block = es.enter_context(nc.Block())

        ps_s = [pool[:, 0:1024], pool[:, 1024:2048]]
        ps_ec = pool[0:36, 2048:3072]
        ps_q = pool[:, 3072:4096]
        ps_m = pool[0:32, 3072:4096]
        ps_zb = pool[0:4, 3072:4096]
        ps_o = pool[:, 3072:4096]
        ps_n = pool[0:1, 0:1024]
        ps_mr = pool[0:1, 0:1024]
        ps_wo = [pool[:, 0:1024], pool[:, 1024:2048]]

        t_qall = 1
        t_norm = lambda i: 2 + i
        t_msq = 10
        sc_done = {}
        misc = {}
        ec_done = {}
        zb_done = {}
        wv_done = {}
        wo_done = {}
        tctr = [0]

        v_ckv5, v_qall, v_max1, v_max2, v_ln, v_lncp, v_mrow = 1, 2, 11, 12, 13, 14, 15
        v_norm = lambda i: 3 + i
        v_recip = lambda h: 16 + 4 * h
        v_zib = lambda h: 17 + 4 * h
        v_rhsv = lambda h: 18 + 4 * h
        v_oev = lambda h: 19 + 4 * h
        v_head_end = 15 + 4 * NH
        v_woev = lambda j: v_head_end + j + 1

        a_sq4, a_qsq, a_ln, a_m = 1, 2, 3, 4
        a_exp = lambda gi: 5 + gi

        g_pre = 16 * 9
        g_bounce = 16 * 11
        g_mrow = 16 * 13
        g_qrow = lambda h: 16 * (13 + h + 1)
        g_wol = lambda j: 16 * (45 + j + 1)

        s_sto = lambda j: 16 * (j + 1)

        @block.tensor
        def _(t):
            def tinc(mm):
                tctr[0] += 1
                mm.then_inc(TS_, 1)
                return tctr[0]
            t.wait_ge(GD_, g_pre)
            t.matmul(ps_q[:, 0:512], wqk_s[:], cq_s[:, 0:512], start=True, stop=True)
            tinc(t.matmul(ps_q[:, 512:1024], wqk_s[:], cq_s[:, 512:1024], start=True, stop=True))
            t.wait_ge(AS_, a_sq4)
            for i in range(8):
                if i >= 1:
                    t.wait_ge(VS_, v_norm(i - 1))
                t.matmul(ps_n[:, 0:512], onescb[0:4, :], ckv_s[:, i * 1024:i * 1024 + 512],
                         start=True, stop=True)
                tinc(t.matmul(ps_n[:, 512:1024], onescb[0:4, :],
                              ckv_s[:, i * 1024 + 512:(i + 1) * 1024], start=True, stop=True))
            t.wait_ge(VS_, v_qall)
            t.wait_ge(AS_, a_qsq)
            t.matmul(ps_m[:, 0:512], blkdg[:], qsq[:, 0:512], start=True, stop=True)
            tinc(t.matmul(ps_m[:, 512:1024], blkdg[:], qsq[:, 512:1024], start=True, stop=True))
            t.wait_ge(AS_, a_m)
            t.wait_ge(VS_, v_max1)
            t.matmul(ps_mr[:, 0:512], sstat[0:1, 6:7], onesr_f[0:1, 0:512], start=True, stop=True)
            misc['t_mrowbc'] = tinc(t.matmul(ps_mr[:, 512:1024], sstat[0:1, 6:7],
                                             onesr_f[0:1, 512:1024], start=True, stop=True))

            def head_tail(h):
                t.wait_ge(VS_, v_recip(h))
                t.matmul(ps_zb[:, 0:512], ones_f[0:1, 0:4], zi[0:1, 0:512], start=True, stop=True)
                zb_done[h] = tinc(t.matmul(ps_zb[:, 512:1024], ones_f[0:1, 0:4],
                                           zi[0:1, 512:1024], start=True, stop=True))
                t.wait_ge(VS_, v_rhsv(h))
                lw = wvu_s[:, h * 128:(h + 1) * 128]
                t.matmul(ps_o[:, 0:512], lw, rhsv[:, 0:512], start=True, stop=True)
                wv_done[h] = tinc(t.matmul(ps_o[:, 512:1024], lw, rhsv[:, 512:1024],
                                           start=True, stop=True))

            for h in range(NH):
                if h >= 1:
                    head_tail(h - 1)
                t.wait_ge(GD_, g_qrow(h))
                rhs = rhsb[:, (h % 2) * TOK:(h % 2 + 1) * TOK]
                for kc in range(NKC):
                    gi = h * NKC + kc
                    pb = ps_s[gi % 2]
                    if gi >= 2:
                        t.wait_ge(AS_, a_exp(gi - 2))
                    lhs5 = ckvT5[:, kc * 128:(kc + 1) * 128]
                    if gi == 0:
                        t.wait_ge(VS_, v_ckv5)
                    t.matmul(pb[:, 0:512], lhs5, rhs[:, 0:512], start=True, stop=True)
                    sc_done[gi] = tinc(t.matmul(pb[:, 512:1024], lhs5, rhs[:, 512:1024],
                                                start=True, stop=True))
                    t.wait_ge(AS_, a_exp(gi))
                    if kc == 0:
                        t.wait_ge(VS_, v_rhsv(h - 1) if h >= 1 else 0)
                    lc1 = c1_s[:, kc * 36:(kc + 1) * 36]
                    E = Eb[:, (gi % 2) * TOK:(gi % 2 + 1) * TOK]
                    t.matmul(ps_ec[:, 0:512], lc1, E[:, 0:512],
                             start=(kc == 0), stop=(kc == NKC - 1))
                    ec_done[gi] = tinc(t.matmul(ps_ec[:, 512:1024], lc1, E[:, 512:1024],
                                                start=(kc == 0), stop=(kc == NKC - 1)))
            head_tail(NH - 1)
            t.wait_ge(AS_, a_exp(NH * NKC - 1))
            t.wait_ge(VS_, v_oev(NH - 1))
            for og in range(8):
                for ot in range(4):
                    j = og * 4 + ot
                    pw = ps_wo[j % 2]
                    if j >= 2:
                        t.wait_ge(VS_, v_woev(j - 2))
                    for half in range(2):
                        t.wait_ge(GD_, g_wol(og * 2 + half))
                        for k2 in range(16):
                            lt = wos[:, (half % 2) * 8192 + k2 * 512 + ot * 128:
                                        (half % 2) * 8192 + k2 * 512 + ot * 128 + 128]
                            st = (half == 0 and k2 == 0)
                            sp = (half == 1 and k2 == 15)
                            kin = half * 16 + k2
                            sl = opre[:, kin * TOK:(kin + 1) * TOK]
                            t.matmul(pw[:, 0:512], lt, sl[:, 0:512], start=st, stop=sp)
                            mm = t.matmul(pw[:, 512:1024], lt, sl[:, 512:1024], start=st, stop=sp)
                            if sp:
                                wo_done[j] = tinc(mm)

        @block.gpsimd
        def _(g):
            g.memset(ckvT5[:], 1.0)
            g.memset(onesr_f[:], 1.0)
            g.memset(ones_f[:], 1.0)
            g.memset(onescf[:], 1.0)
            g.memset(onescb[:], 1.0).then_inc(GP_, 1)
            g.dma_start(out=cq_s[:], in_=cqT).then_inc(GD_, 16)
            g.dma_start(out=wqk_s[:], in_=wqk).then_inc(GD_, 16)
            g.dma_start(out=bqk_s[:], in_=bqk).then_inc(GD_, 16)
            g.dma_start(out=ckv_s[:], in_=ckvT).then_inc(GD_, 16)
            g.dma_start(out=c1_s[:], in_=c1).then_inc(GD_, 16)
            g.dma_start(out=wvu_s[:], in_=wvu).then_inc(GD_, 16)
            g.dma_start(out=bvu_s[:], in_=bvu).then_inc(GD_, 16)
            g.dma_start(out=bo_s[:], in_=bo).then_inc(GD_, 16)
            g.dma_start(out=blkdg[:], in_=bd).then_inc(GD_, 16)
            g.wait_ge(VS_, v_max1)
            g.dma_start(out=scr32.rearrange("(a b) -> a b", a=32), in_=msqcol[:]).then_inc(GD_, 16)
            g.wait_ge(GD_, g_bounce - 16)
            g.dma_start(out=dr32[:], in_=scr32.rearrange("(a b) -> b a", b=1)).then_inc(GD_, 16)
            g.wait_ge(VS_, v_mrow)
            g.dma_start(out=rhsb[4:5, 0:TOK], in_=mrow[:]).then_inc(GD_, 16)
            g.dma_start(out=rhsb[4:5, TOK:2 * TOK], in_=mrow[:]).then_inc(GD_, 16)
            for h in range(NH):
                if h >= 2:
                    g.wait_ge(TS_, sc_done[(h - 2) * NKC + NKC - 1])
                g.wait_ge(VS_, v_qall)
                g.wait_ge(GD_, g_qrow(h) - 16)
                g.dma_start(out=rhsb[0:4, (h % 2) * TOK:(h % 2 + 1) * TOK],
                            in_=qall[4 * h:4 * h + 4, :]).then_inc(GD_, 16)
            g.wait_ge(GD_, g_qrow(NH - 1))
            for og in range(8):
                for half in range(2):
                    j = og * 2 + half
                    if j >= 2:
                        g.wait_ge(TS_, wo_done[(og - 1) * 4 + 3])
                    g.dma_start(
                        out=wos[:, (half % 2) * 8192:(half % 2) * 8192 + 8192].rearrange(
                            "p (c o) -> p c o", c=16),
                        in_=wo[half * 2048 + og * 0:0 + half * 2048 + 2048,
                               og * 512:(og + 1) * 512].rearrange(
                            "(c p) o -> p c o", p=128)).then_inc(GD_, 16)

        @block.scalar
        def _(a):
            a.wait_ge(VS_, v_ckv5)
            a.activation(ckv_s[:], ckvT5[0:4, :], AF.Square).then_inc(AS_, 1)
            a.wait_ge(VS_, v_qall)
            a.activation(qsq[:], qall[:], AF.Square).then_inc(AS_, 1)
            a.wait_ge(VS_, v_max2)
            a.activation(sstat[0:1, 4:6], sstat[0:1, 0:2], AF.Ln).then_inc(AS_, 1)
            a.wait_ge(VS_, v_lncp)
            a.activation(sstat[0:1, 6:8], sstat[0:1, 2:4], AF.Exp, scale=0.5).then_inc(AS_, 1)
            for h in range(NH):
                for kc in range(NKC):
                    gi = h * NKC + kc
                    a.wait_ge(TS_, sc_done[gi])
                    if gi >= 2:
                        a.wait_ge(TS_, ec_done[gi - 2])
                    pb = ps_s[gi % 2]
                    E = Eb[:, (gi % 2) * TOK:(gi % 2 + 1) * TOK]
                    a.activation(E, pb, AF.Exp).then_inc(AS_, 1)

        @block.vector
        def _(v):
            v.wait_ge(GP_, 1)
            v.wait_ge(GD_, g_pre)
            v.tensor_scalar(ckvT5[0:4, :], ckv_s[:], 0.5, None, ALU.mult).then_inc(VS_, 1)
            v.wait_ge(TS_, t_qall)
            v.tensor_scalar(qall[:], ps_q, bqk_s[:], None, ALU.add).then_inc(VS_, 1)
            for i in range(8):
                v.wait_ge(TS_, t_norm(i))
                v.reduce_max(sstat[0:1, 8 + i:9 + i], ps_n, axis=mybir.AxisListType.X).then_inc(VS_, 1)
            v.wait_ge(TS_, t_msq)
            v.reduce_max(sstat[0:1, 0:1], sstat[0:1, 8:16], axis=mybir.AxisListType.X)
            v.reduce_max(msqcol[:], ps_m, axis=mybir.AxisListType.X).then_inc(VS_, 1)
            v.wait_ge(GD_, g_bounce)
            v.reduce_max(sstat[0:1, 1:2], dr32[:], axis=mybir.AxisListType.X).then_inc(VS_, 1)
            v.wait_ge(AS_, a_ln)
            v.reduce_sum(sstat[0:1, 2:3], sstat[0:1, 4:6], axis=mybir.AxisListType.X).then_inc(VS_, 1)
            v.wait_ge(VS_, v_ln)
            v.tensor_copy(sstat[0:1, 3:4], sstat[0:1, 2:3]).then_inc(VS_, 1)
            v.wait_ge(TS_, misc['t_mrowbc'])
            v.tensor_scalar(mrow[:], ps_mr, -1.0, None, ALU.mult).then_inc(VS_, 1)
            for h in range(NH):
                v.wait_ge(TS_, ec_done[h * NKC + NKC - 1])
                v.reciprocal(zi[:], ps_ec[32:33, 0:1024]).then_inc(VS_, 1)
                v.wait_ge(TS_, zb_done[h])
                v.tensor_copy(zib[:], ps_zb).then_inc(VS_, 1)
                v.tensor_tensor(rhsv[:], ps_ec[0:4, 0:1024], zib[:], ALU.mult).then_inc(VS_, 1)
                v.wait_ge(TS_, wv_done[h])
                v.tensor_scalar(opre[:, h * TOK:(h + 1) * TOK], ps_o, bvu_s[:, h:h + 1],
                                None, ALU.add).then_inc(VS_, 1)
            for j in range(32):
                og, ot = j // 4, j % 4
                v.wait_ge(TS_, wo_done[j])
                if j >= 2:
                    v.wait_ge(SD_, s_sto(j - 2))
                v.tensor_scalar(oev[:, (j % 2) * TOK:(j % 2 + 1) * TOK], ps_wo[j % 2],
                                bo_s[:, j:j + 1], None, ALU.add).then_inc(VS_, 1)

        @block.sync
        def _(s):
            for j in range(32):
                s.wait_ge(VS_, v_woev(j))
                s.dma_start(out=aout[j * 128:(j + 1) * 128, :],
                            in_=oev[:, (j % 2) * TOK:(j % 2 + 1) * TOK]).then_inc(SD_, 16)
            s.wait_ge(SD_, s_sto(31))
    es.close()
    return nc


def stage_l0b_inputs(cq_all, ckv_all, wqk, bqk, wvu, bvu, wo, bo):
    ckvT = np.ascontiguousarray(ckv_all.T)
    c1 = np.zeros((128, NKC * 36), np.float32)
    for kc in range(NKC):
        c1[:, kc * 36:kc * 36 + 4] = ckv_all[kc * 128:(kc + 1) * 128, :]
        c1[:, kc * 36 + 32] = 1.0
    bd = np.kron(np.eye(NH, dtype=np.float32), np.ones((4, 1), np.float32))
    common = {
        "wqk": wqk, "bqk": bqk.reshape(128, 1), "ckvT": ckvT, "c1": c1,
        "wvu": wvu, "bvu": np.ascontiguousarray(bvu.reshape(DCH, 128).T),
        "wo": wo, "bo": np.ascontiguousarray(bo.reshape(DCH, 128).T), "bd": bd,
    }
    maps = []
    for c in range(NCORES):
        m = dict(common)
        m["cqT"] = np.ascontiguousarray(cq_all[c * TOK:(c + 1) * TOK, :].T)
        maps.append(m)
    return maps



PREP_SPECS = [
    ("gu", 4096, 2752, 32, False),
    ("dn", 11008, 512, 86, False),
    ("gate", 4096, 1376, 32, True),
    ("up", 4096, 1376, 32, True),
    ("down", 11008, 512, 86, True),
]


def build_prep():
    nc = bass.Bass("TRN2", target_bir_lowering=False, debug=False)
    ins = {}
    outs = {}
    for name, K, NSL, G, gptq in PREP_SPECS:
        qwcols = NSL if gptq else NSL // 8
        ins[name] = (
            nc.dram_tensor(f"{name}_qw", [K, qwcols], I32, kind="ExternalInput").ap(),
            nc.dram_tensor(f"{name}_qz", [G, NSL // 8], I32, kind="ExternalInput").ap(),
            nc.dram_tensor(f"{name}_sc", [G, NSL], F32, kind="ExternalInput").ap(),
        )
        outs[name] = nc.dram_tensor(f"{name}_w", [K, NSL], BF16, kind="ExternalOutput").ap()
    idm = nc.dram_tensor("idm", [128, 128], F32, kind="ExternalInput").ap()
    sha = nc.dram_tensor("sha", [128, 1], I32, kind="ExternalInput").ap()

    es = contextlib.ExitStack()
    sb = lambda name, shape, dt=F32: es.enter_context(nc.sbuf_tensor(name, shape, dt))
    sem = lambda name: es.enter_context(nc.semaphore(name))
    if True:
        qwb = sb("qwb", [128, 2 * 1376], I32)
        w4i = sb("w4i", [128, 2752], I32)
        w4f = sb("w4f", [128, 2752])
        zi_ = sb("zi", [86, 2752], I32)
        zf_ = sb("zf", [86, 2752])
        sf_ = sb("sf", [86, 2752])
        scr = sb("scr", [128, 512])
        ob = sb("ob", [128, 2 * 2752], BF16)
        ids = sb("ids", [128, 128])
        shs = sb("shs", [128, 1], I32)
        pool = es.enter_context(nc.psum_tensor("pool", [128, 4096], F32))
        GD_, VS_, TS_, SD_ = sem("gd"), sem("vs"), sem("ts"), sem("sd")
        block = es.enter_context(nc.Block())

        zb = [pool[:, 0:512], pool[:, 1024:1536]]
        sb_ = [pool[:, 512:1024], pool[:, 1536:2048]]

        gd_n = [2 * 16]
        vs_n = [0]
        ts_n = [0]
        sd_n = [0]
        tab = {}
        for name, K, NSL, G, gptq in PREP_SPECS:
            nb = (NSL + 511) // 512
            t = {"nb": nb}
            t["gd_zs"] = gd_n[0] + 32
            gd_n[0] += 32
            t["gd_qw"] = [gd_n[0] + 16 * (g + 1) for g in range(G)]
            gd_n[0] += 16 * G
            t["vs_z"] = vs_n[0] + 1
            vs_n[0] += 1
            t["vs_unit"] = {}
            t["ts_unit"] = {}
            u = 0
            for g in range(G):
                for b in range(nb):
                    ts_n[0] += 1
                    t["ts_unit"][u] = ts_n[0]
                    u += 1
            u = 0
            for g in range(G):
                for b in range(nb):
                    vs_n[0] += 1
                    t["vs_unit"][u] = vs_n[0]
                    u += 1
            t["vs_gend"] = lambda g, t=t: t["vs_unit"][(g + 1) * t["nb"] - 1]
            t["sd_g"] = [sd_n[0] + 16 * (g + 1) for g in range(G)]
            sd_n[0] += 16 * G
            tab[name] = t

        @block.tensor
        def _(t):
            t.wait_ge(GD_, 16)
            for name, K, NSL, G, gptq in PREP_SPECS:
                T = tab[name]
                nb = T["nb"]
                t.wait_ge(VS_, T["vs_z"])
                for g in range(G):
                    for b in range(nb):
                        u = g * nb + b
                        w = min(512, NSL - b * 512)
                        if u >= 2:
                            t.wait_ge(VS_, T["vs_unit"][u - 2])
                        lhs = ids[0:G, g:g + 1].to_broadcast([G, 128])
                        t.matmul(zb[u % 2][:, 0:w], lhs, zf_[0:G, b * 512:b * 512 + w],
                                 start=True, stop=True)
                        t.matmul(sb_[u % 2][:, 0:w], lhs, sf_[0:G, b * 512:b * 512 + w],
                                 start=True, stop=True).then_inc(TS_, 1)

        prev_tab = [None]

        @block.gpsimd
        def _(g):
            g.dma_start(out=ids[:], in_=idm).then_inc(GD_, 16)
            g.dma_start(out=shs[:], in_=sha).then_inc(GD_, 16)
            for name, K, NSL, G, gptq in PREP_SPECS:
                T = tab[name]
                qw, qz, sc = ins[name]
                if name != "gu":
                    g.wait_ge(VS_, prev_tab[0]["vs_gend"](prev_tab[0]["G"] - 1))
                g.dma_start(out=sf_[0:G, 0:NSL], in_=sc).then_inc(GD_, 16)
                g.dma_start(out=zi_[0:G, 0:NSL // 8], in_=qz).then_inc(GD_, 16)
                g.wait_ge(GD_, T["gd_zs"])
                qwcols = NSL if gptq else NSL // 8
                for gi in range(G):
                    if gi >= 2:
                        g.wait_ge(VS_, T["vs_gend"](gi - 2))
                    g.dma_start(out=qwb[:, (gi % 2) * qwcols:(gi % 2 + 1) * qwcols],
                                in_=qw[gi * 128:(gi + 1) * 128, :]).then_inc(GD_, 16)
                prev_tab[0] = {"vs_gend": T["vs_gend"], "G": G}

        @block.vector
        def _(v):
            for name, K, NSL, G, gptq in PREP_SPECS:
                T = tab[name]
                nb = T["nb"]
                v.wait_ge(GD_, T["gd_zs"])
                for i in range(8):
                    if gptq:
                        dst = w4i[0:G, 0:NSL].rearrange("g (m e) -> g m e", e=8)[:, :, i:i + 1]
                    else:
                        dst = w4i[0:G, i * (NSL // 8):(i + 1) * (NSL // 8)]
                    v.tensor_scalar(dst, zi_[0:G, 0:NSL // 8], 4 * i, 15,
                                    ALU.logical_shift_right, ALU.bitwise_and)
                v.tensor_copy(zf_[0:G, 0:NSL], w4i[0:G, 0:NSL])
                if gptq:
                    v.tensor_scalar(zf_[0:G, 0:NSL], zf_[0:G, 0:NSL], 1.0, None, ALU.add)
                v.engine_nop().then_inc(VS_, 1)
                qwcols = NSL if gptq else NSL // 8
                for gi in range(G):
                    v.wait_ge(GD_, T["gd_qw"][gi])
                    qwt = qwb[:, (gi % 2) * qwcols:(gi % 2 + 1) * qwcols]
                    if gptq:
                        v.tensor_scalar(w4i[:, 0:NSL], qwt, shs[:], 15,
                                        ALU.logical_shift_right, ALU.bitwise_and)
                    else:
                        for i in range(8):
                            v.tensor_scalar(w4i[:, i * (NSL // 8):(i + 1) * (NSL // 8)],
                                            qwt, 4 * i, 15,
                                            ALU.logical_shift_right, ALU.bitwise_and)
                    v.tensor_copy(w4f[:, 0:NSL], w4i[:, 0:NSL])
                    if gi >= 2:
                        v.wait_ge(SD_, T["sd_g"][gi - 2])
                    for b in range(nb):
                        u = gi * nb + b
                        w = min(512, NSL - b * 512)
                        v.wait_ge(TS_, T["ts_unit"][u])
                        v.tensor_tensor(scr[:, 0:w], w4f[:, b * 512:b * 512 + w],
                                        zb[u % 2][:, 0:w], ALU.subtract)
                        v.tensor_tensor(ob[:, (gi % 2) * NSL + b * 512:(gi % 2) * NSL + b * 512 + w],
                                        scr[:, 0:w], sb_[u % 2][:, 0:w],
                                        ALU.mult).then_inc(VS_, 1)

        @block.sync
        def _(s):
            total = 0
            for name, K, NSL, G, gptq in PREP_SPECS:
                T = tab[name]
                for gi in range(G):
                    s.wait_ge(VS_, T["vs_gend"](gi))
                    s.dma_start(out=outs[name][gi * 128:(gi + 1) * 128, :],
                                in_=ob[:, (gi % 2) * NSL:(gi % 2) * NSL + NSL]).then_inc(SD_, 16)
                    total += 16
            s.wait_ge(SD_, total)
    es.close()
    return nc


def stage_prep_inputs(d, core):
    c = core
    m = {"idm": np.eye(128, dtype=np.float32),
         "sha": (np.arange(128, dtype=np.int32) % 8 * 4).reshape(128, 1)}
    jsl = slice(344 * c, 344 * (c + 1))
    m["gu_qw"] = np.ascontiguousarray(d["L0_ffn_qweight_gu"][:, jsl])
    m["gu_qz"] = np.ascontiguousarray(d["L0_ffn_qzeros_gu"][:, jsl])
    perm = np.array([8 * j + i for i in range(8) for j in range(344 * c, 344 * (c + 1))])
    m["gu_sc"] = np.ascontiguousarray(d["L0_ffn_scales_gu"][:, perm])
    jsl = slice(64 * c, 64 * (c + 1))
    m["dn_qw"] = np.ascontiguousarray(d["L0_ffn_qweight_dn"][:, jsl])
    m["dn_qz"] = np.ascontiguousarray(d["L0_ffn_qzeros_dn"][:, jsl])
    perm = np.array([8 * j + i for i in range(8) for j in range(64 * c, 64 * (c + 1))])
    m["dn_sc"] = np.ascontiguousarray(d["L0_ffn_scales_dn"][:, perm])
    nsl = slice(1376 * c, 1376 * (c + 1))
    for nm, pre in (("gate", "L1_ffn_gate"), ("up", "L1_ffn_up")):
        m[f"{nm}_qw"] = np.ascontiguousarray(np.repeat(d[f"{pre}_qweight"], 8, axis=0)[:, nsl])
        m[f"{nm}_qz"] = np.ascontiguousarray(d[f"{pre}_qzeros"][:, slice(172 * c, 172 * (c + 1))])
        m[f"{nm}_sc"] = np.ascontiguousarray(d[f"{pre}_scales"][:, nsl])
    nsl = slice(512 * c, 512 * (c + 1))
    m["down_qw"] = np.ascontiguousarray(np.repeat(d["L1_ffn_down_qweight"], 8, axis=0)[:, nsl])
    m["down_qz"] = np.ascontiguousarray(d["L1_ffn_down_qzeros"][:, slice(64 * c, 64 * (c + 1))])
    m["down_sc"] = np.ascontiguousarray(d["L1_ffn_down_scales"][:, nsl])
    return m


def host_dequant_ref(d):
    def awq(qw, qz, sc, gsz=128):
        K = qw.shape[0]
        sh = (np.arange(8, dtype=np.int32) * 4)
        w = ((qw[:, :, None] >> sh) & 15).reshape(K, -1).astype(np.float32)
        z = ((qz[:, :, None] >> sh) & 15).reshape(qz.shape[0], -1).astype(np.float32)
        return (w - np.repeat(z, gsz, 0)) * np.repeat(sc, gsz, 0)
    def gptq(qw, qz, sc, gsz=128):
        sh = (np.arange(8, dtype=np.uint32) * 4)
        q = qw.astype(np.uint32)
        w = ((q[:, None, :] >> sh[None, :, None]) & 15).reshape(-1, qw.shape[1]).astype(np.float32)
        z = (((qz.astype(np.uint32)[:, :, None] >> sh[None, None, :]) & 15) + 1).reshape(qz.shape[0], -1).astype(np.float32)
        gi = np.arange(w.shape[0]) // gsz
        return (w - z[gi]) * sc[gi]
    return {
        "W_gu": awq(d["L0_ffn_qweight_gu"], d["L0_ffn_qzeros_gu"], d["L0_ffn_scales_gu"]),
        "W_dn": awq(d["L0_ffn_qweight_dn"], d["L0_ffn_qzeros_dn"], d["L0_ffn_scales_dn"]),
        "W_gate": gptq(d["L1_ffn_gate_qweight"], d["L1_ffn_gate_qzeros"], d["L1_ffn_gate_scales"]),
        "W_up": gptq(d["L1_ffn_up_qweight"], d["L1_ffn_up_qzeros"], d["L1_ffn_up_scales"]),
        "W_down": gptq(d["L1_ffn_down_qweight"], d["L1_ffn_down_qzeros"], d["L1_ffn_down_scales"]),
    }



NIT = 86
NIC = 86


def build_l0c():
    nc = bass.Bass("TRN2", target_bir_lowering=False, debug=False)
    xT = nc.dram_tensor("xT", [D, TOK], F32, kind="ExternalInput").ap()
    aout = nc.dram_tensor("aout", [D, TOK], F32, kind="ExternalInput").ap()
    xn = nc.dram_tensor("xn", [D, TOK], BF16, kind="ExternalInput").ap()
    gu = nc.dram_tensor("gu", [NIT, D, 256], BF16, kind="ExternalInput").ap()
    dnw = nc.dram_tensor("dnw", [4, 11008, 1024], BF16, kind="ExternalInput").ap()
    x1 = nc.dram_tensor("x1", [D, TOK], F32, kind="ExternalOutput").ap()
    hscr = nc.dram_tensor("hscr", [11008, TOK], BF16, kind="ExternalOutput").ap()

    es = contextlib.ExitStack()
    sb = lambda name, shape, dt=F32: es.enter_context(nc.sbuf_tensor(name, shape, dt))
    sem = lambda name: es.enter_context(nc.semaphore(name))
    if True:
        xns = sb("xns", [128, DCH * TOK], BF16)
        gub = sb("gub", [128, 2 * 32 * 256], BF16)
        gsil = sb("gsil", [128, 2 * TOK])
        htl = sb("htl", [128, 2 * TOK], BF16)
        dnb = sb("dnb", [128, 2 * 1024], BF16)
        hib = sb("hib", [128, 2 * 512], BF16)
        xcb = sb("xcb", [128, 8 * 512])
        acb = sb("acb", [128, 8 * 512])
        t1b = sb("t1b", [128, 512])
        x1b = sb("x1b", [128, 8 * 512])
        pool = es.enter_context(nc.psum_tensor("pool", [128, 4096], F32))
        GD_, SY_, TS_, AS_, VS_ = sem("gd"), sem("sy"), sem("ts"), sem("as"), sem("vs")
        GUB = [sem("gub0"), sem("gub1")]
        DNB = [sem("dnb0"), sem("dnb1")]
        HIB = [sem("hib0"), sem("hib1")]
        HST = [sem("hst0"), sem("hst1")]
        block = es.enter_context(nc.Block())

        a_sil = {it: it + 1 for it in range(NIT)}
        v_h = {it: it + 1 for it in range(NIT)}
        v_ev = {}
        c = NIT
        for u in range(8):
            for ot4 in range(8):
                c += 1
                v_ev[(u, ot4)] = c
        g_xn = 16
        g_gu = lambda it: 16 * (1 + it + 1)
        g_p3 = 16 * (1 + NIT)
        g_dnh = lambda u, ic: g_p3 + 16 * 2 * (u * NIC + ic + 1)
        sy_xa = lambda u: 16 * (24 * u + 16)
        sy_st = lambda u, ot4: 16 * (24 * u + 16 + ot4 + 1)
        t_gu = {}
        t_ic = {}
        t_dn = {}
        tctr = [0]

        @block.tensor
        def _(t):
            def tinc(mm):
                tctr[0] += 1
                mm.then_inc(TS_, 1)
                return tctr[0]
            t.wait_ge(GD_, g_xn)
            for it in range(NIT):
                b = it % 2
                gps = pool[:, b * 2048:b * 2048 + 1024]
                ups = pool[:, b * 2048 + 1024:b * 2048 + 2048]
                t.wait_ge(GUB[it % 2], 16 * (it // 2 + 1))
                if it >= 2:
                    t.wait_ge(AS_, a_sil[it - 2])
                    t.wait_ge(VS_, v_h[it - 2])
                for kc in range(DCH):
                    lt = gub[:, b * 8192 + kc * 256:b * 8192 + kc * 256 + 256]
                    xc = xns[:, kc * TOK:(kc + 1) * TOK]
                    st, sp = kc == 0, kc == DCH - 1
                    t.matmul(gps[:, 0:512], lt[:, 0:128], xc[:, 0:512], start=st, stop=sp)
                    t.matmul(gps[:, 512:1024], lt[:, 0:128], xc[:, 512:1024], start=st, stop=sp)
                    t.matmul(ups[:, 0:512], lt[:, 128:256], xc[:, 0:512], start=st, stop=sp)
                    mm = t.matmul(ups[:, 512:1024], lt[:, 128:256], xc[:, 512:1024], start=st, stop=sp)
                t_gu[it] = tinc(mm)
            t.wait_ge(AS_, a_sil[NIT - 1])
            t.wait_ge(VS_, v_h[NIT - 1])
            for u in range(8):
                th, rep = u // 4, u % 4
                if u >= 1:
                    t.wait_ge(VS_, v_ev[(u - 1, 7)])
                for ic in range(NIC):
                    j = u * NIC + ic
                    t.wait_ge(DNB[j % 2], 16 * (j // 2 + 1))
                    t.wait_ge(HIB[j % 2], 16 * (j // 2 + 1))
                    db = dnb[:, (ic % 2) * 1024:(ic % 2 + 1) * 1024]
                    hb = hib[:, (ic % 2) * 512:(ic % 2 + 1) * 512]
                    for ot4 in range(8):
                        mm = t.matmul(pool[:, ot4 * 512:(ot4 + 1) * 512],
                                      db[:, ot4 * 128:(ot4 + 1) * 128], hb,
                                      start=(ic == 0), stop=(ic == NIC - 1))
                    t_ic[(u, ic)] = tinc(mm)
                for ot4 in range(8):
                    t_dn[(u, ot4)] = t_ic[(u, NIC - 1)]

        @block.gpsimd
        def _(g):
            load_chunked(g, xns[:], xn, DCH, TOK).then_inc(GD_, 16)
            for it in range(NIT):
                if it >= 2:
                    g.wait_ge(TS_, t_gu[it - 2])
                g.dma_start(out=gub[:, (it % 2) * 8192:(it % 2 + 1) * 8192].rearrange(
                                "p (c n) -> p c n", c=32),
                            in_=gu[it].rearrange("(c p) n -> p c n", p=128)).then_inc(GUB[it % 2], 16)
            for u in range(8):
                th, rep = u // 4, u % 4
                for ic in range(NIC):
                    li = u * NIC + ic - 2
                    if li >= 0:
                        g.wait_ge(TS_, t_ic[(li // NIC, li % NIC)])
                    j = u * NIC + ic
                    g.dma_start(out=dnb[:, (ic % 2) * 1024:(ic % 2 + 1) * 1024],
                                in_=dnw[rep, ic * 128:(ic + 1) * 128, :]).then_inc(DNB[j % 2], 16)
                    if u == 0:
                        g.wait_ge(HST[ic % 2], 16 * (ic // 2 + 1))
                    g.dma_start(out=hib[:, (ic % 2) * 512:(ic % 2 + 1) * 512],
                                in_=hscr[ic * 128:(ic + 1) * 128,
                                         th * 512:(th + 1) * 512]).then_inc(HIB[j % 2], 16)

        @block.scalar
        def _(a):
            for it in range(NIT):
                b = it % 2
                a.wait_ge(TS_, t_gu[it])
                if it >= 2:
                    a.wait_ge(VS_, v_h[it - 2])
                gps = pool[:, b * 2048:b * 2048 + 1024]
                a.activation(gsil[:, b * TOK:(b + 1) * TOK], gps, AF.Silu).then_inc(AS_, 1)

        @block.vector
        def _(v):
            for it in range(NIT):
                b = it % 2
                v.wait_ge(AS_, a_sil[it])
                if it >= 2:
                    v.wait_ge(HST[it % 2], 16 * ((it - 2) // 2 + 1))
                ups = pool[:, b * 2048 + 1024:b * 2048 + 2048]
                v.tensor_tensor(htl[:, b * TOK:(b + 1) * TOK], ups,
                                gsil[:, b * TOK:(b + 1) * TOK], ALU.mult).then_inc(VS_, 1)
            for u in range(8):
                th, rep = u // 4, u % 4
                for ot4 in range(8):
                    v.wait_ge(TS_, t_dn[(u, ot4)])
                    v.wait_ge(SY_, sy_xa(u))
                    if u >= 1:
                        v.wait_ge(SY_, 16 * (24 * (u - 1) + 24))
                    v.tensor_tensor(t1b[:], pool[:, ot4 * 512:(ot4 + 1) * 512],
                                    xcb[:, ot4 * 512:(ot4 + 1) * 512], ALU.add)
                    v.tensor_tensor(x1b[:, ot4 * 512:(ot4 + 1) * 512], t1b[:],
                                    acb[:, ot4 * 512:(ot4 + 1) * 512], ALU.add).then_inc(VS_, 1)

        @block.sync
        def _(s):
            for it in range(NIT):
                s.wait_ge(VS_, v_h[it])
                s.dma_start(out=hscr[it * 128:(it + 1) * 128, :],
                            in_=htl[:, (it % 2) * TOK:(it % 2 + 1) * TOK]).then_inc(HST[it % 2], 16)
            for u in range(8):
                th, rep = u // 4, u % 4
                if u >= 1:
                    s.wait_ge(VS_, v_ev[(u - 1, 7)])
                for ot4 in range(8):
                    oc = 8 * rep + ot4
                    s.dma_start(out=xcb[:, ot4 * 512:(ot4 + 1) * 512],
                                in_=xT[oc * 128:(oc + 1) * 128,
                                       th * 512:(th + 1) * 512]).then_inc(SY_, 16)
                    s.dma_start(out=acb[:, ot4 * 512:(ot4 + 1) * 512],
                                in_=aout[oc * 128:(oc + 1) * 128,
                                         th * 512:(th + 1) * 512]).then_inc(SY_, 16)
                s.wait_ge(SY_, 16 * (24 * u + 16))
                for ot4 in range(8):
                    oc = 8 * rep + ot4
                    s.wait_ge(VS_, v_ev[(u, ot4)])
                    s.dma_start(out=x1[oc * 128:(oc + 1) * 128, th * 512:(th + 1) * 512],
                                in_=x1b[:, ot4 * 512:(ot4 + 1) * 512]).then_inc(SY_, 16)
                s.wait_ge(SY_, 16 * (24 * u + 24))
            pass
    es.close()
    return nc


def stage_l0c_inputs_weights(gu_slices, dn_slices):
    gate_full = np.concatenate(gu_slices[:4], axis=1)
    up_full = np.concatenate(gu_slices[4:], axis=1)
    gu_blk = np.empty((NIT, D, 256), dtype=gate_full.dtype)
    for it in range(NIT):
        gu_blk[it, :, 0:128] = gate_full[:, it * 128:(it + 1) * 128]
        gu_blk[it, :, 128:256] = up_full[:, it * 128:(it + 1) * 128]
    dn_asm = np.concatenate(dn_slices, axis=1)
    n = np.arange(4096)
    pos = 512 * ((n // 8) // 64) + 64 * (n % 8) + (n // 8) % 64
    dn_nat = dn_asm[:, pos]
    p = np.arange(11008)
    c = p // 2752
    i = (p % 2752) // 344
    jl = (p % 2752) % 344
    orig = 8 * (344 * c + jl) + i
    dn_final = dn_nat[orig, :]
    dn_blk = np.ascontiguousarray(dn_final.reshape(11008, 4, 1024).transpose(1, 0, 2))
    return np.ascontiguousarray(gu_blk), dn_blk



NJT = 145


def build_l1a():
    nc = bass.Bass("TRN2", target_bir_lowering=False, debug=False)
    x1T = nc.dram_tensor("x1T", [D, TOK], F32, kind="ExternalInput").ap()
    wpT = nc.dram_tensor("wpT", [D, 18560], F32, kind="ExternalInput").ap()
    xn2o = nc.dram_tensor("xn2", [D, TOK], BF16, kind="ExternalOutput").ap()
    gateo = nc.dram_tensor("gate", [8192, TOK], BF16, kind="ExternalOutput").ap()
    xbcpo = nc.dram_tensor("xbcp", [10240, TOK], BF16, kind="ExternalOutput").ap()
    dto = nc.dram_tensor("dtr", [128, TOK], F32, kind="ExternalOutput").ap()

    es = contextlib.ExitStack()
    sb = lambda name, shape, dt=F32: es.enter_context(nc.sbuf_tensor(name, shape, dt))
    sem = lambda name: es.enter_context(nc.semaphore(name))
    if True:
        sq = sb("sq", [128, 2 * TOK])
        ones1f = sb("ones1f", [1, 128])
        onescf2 = sb("onescf2", [128, 1])
        stats = sb("stats", [1, 6 * TOK])
        Rsb = sb("Rsb", [128, TOK])
        MRsb = sb("MRsb", [128, TOK])
        xns = sb("xns", [128, DCH * TOK], BF16)
        xfc = sb("xfc", [128, 2 * TOK])
        wtb = sb("wtb", [128, 2 * DCH * 128], BF16)
        ob = sb("ob", [128, 2 * TOK], BF16)
        obf = sb("obf", [128, TOK])
        pool = es.enter_context(nc.psum_tensor("pool", [128, 4096], F32))
        GD_, TS_, AS_, VS_, SY_ = sem("gd"), sem("ts"), sem("as"), sem("vs"), sem("sy")
        WTB = [sem("wtb0"), sem("wtb1")]
        OBS = [sem("obs0"), sem("obs1")]
        XF = [sem("xf0"), sem("xf1")]
        XND = sem("xnd")
        block = es.enter_context(nc.Block())

        pm = pool[0:1, 0:1024]
        pv = pool[0:1, 1024:2048]
        pbc = pool[:, 2048:3072]
        pj = [pool[:, 0:1024], pool[:, 1024:2048]]

        a_sq = lambda c: c + 1
        a_sqrt = DCH + 1
        v_scr, v_mr, v_R, v_MR = 1, 3, 4, 5
        v_xn = lambda c: 5 + c + 1
        v_xnend = 5 + DCH
        t_stat = lambda c: c + 1
        t_bc1, t_bc2 = DCH + 1, DCH + 2
        t_proj = lambda jt: DCH + 2 + jt + 1
        v_pe = lambda jt: v_xnend + jt + 1

        @block.gpsimd
        def _(g):
            g.memset(ones1f[:], 1.0)
            g.memset(onescf2[:], 1.0)
            g.engine_nop().then_inc(GD_, 16) if False else None
            for c in range(DCH):
                if c >= 2:
                    g.wait_ge(TS_, t_stat(c - 1))
                g.dma_start(out=xfc[:, (c % 2) * TOK:(c % 2 + 1) * TOK],
                            in_=x1T[c * 128:(c + 1) * 128, :]).then_inc(XF[c % 2], 16)
            for c in range(DCH):
                if c >= 2:
                    g.wait_ge(XND, c - 1)
                else:
                    g.wait_ge(TS_, t_stat(DCH))
                g.dma_start(out=xfc[:, (c % 2) * TOK:(c % 2 + 1) * TOK],
                            in_=x1T[c * 128:(c + 1) * 128, :]).then_inc(XF[c % 2], 16)
            for jt in range(NJT):
                if jt >= 2:
                    g.wait_ge(TS_, t_proj(jt - 2))
                g.dma_start(out=wtb[:, (jt % 2) * 4096:(jt % 2 + 1) * 4096].rearrange(
                                "p (c j) -> p c j", c=DCH),
                            in_=wpT[:, jt * 128:(jt + 1) * 128].rearrange(
                                "(c p) j -> p c j", p=128)).then_inc(WTB[jt % 2], 16)

        @block.scalar
        def _(a):
            for c in range(DCH):
                a.wait_ge(XF[c % 2], 16 * (c // 2 + 1))
                if c >= 2:
                    a.wait_ge(TS_, t_stat(c - 1))
                a.activation(sq[:, (c % 2) * TOK:(c % 2 + 1) * TOK],
                             xfc[:, (c % 2) * TOK:(c % 2 + 1) * TOK], AF.Square).then_inc(AS_, 1)
            a.wait_ge(VS_, v_scr)
            a.activation(stats[0:1, 5 * TOK:6 * TOK], stats[0:1, 4 * TOK:5 * TOK],
                         AF.Sqrt).then_inc(AS_, 1)

        @block.tensor
        def _(t):
            for c in range(DCH):
                b = (c % 2) * TOK
                xc = xfc[:, b:b + TOK]
                sqc = sq[:, b:b + TOK]
                st, sp = (c == 0), (c == DCH - 1)
                t.wait_ge(XF[c % 2], 16 * (c // 2 + 1))
                t.matmul(pm[:, 0:512], onescf2[:], xc[:, 0:512], start=st, stop=sp)
                t.matmul(pm[:, 512:1024], onescf2[:], xc[:, 512:1024], start=st, stop=sp)
                t.wait_ge(AS_, a_sq(c))
                t.matmul(pv[:, 0:512], onescf2[:], sqc[:, 0:512], start=st, stop=sp)
                t.matmul(pv[:, 512:1024], onescf2[:], sqc[:, 512:1024], start=st, stop=sp).then_inc(TS_, 1)
            t.wait_ge(VS_, v_mr)
            t.matmul(pbc[:, 0:512], ones1f[:], stats[0:1, 2 * TOK:2 * TOK + 512], start=True, stop=True)
            t.matmul(pbc[:, 512:1024], ones1f[:], stats[0:1, 2 * TOK + 512:3 * TOK],
                     start=True, stop=True).then_inc(TS_, 1)
            t.wait_ge(VS_, v_R)
            t.matmul(pbc[:, 0:512], ones1f[:], stats[0:1, 3 * TOK:3 * TOK + 512], start=True, stop=True)
            t.matmul(pbc[:, 512:1024], ones1f[:], stats[0:1, 3 * TOK + 512:4 * TOK],
                     start=True, stop=True).then_inc(TS_, 1)
            for jt in range(NJT):
                b = jt % 2
                t.wait_ge(WTB[b], 16 * (jt // 2 + 1))
                if jt >= 2:
                    t.wait_ge(VS_, v_pe(jt - 2))
                for kc in range(DCH):
                    t.wait_ge(VS_, v_xn(kc)) if jt == 0 else None
                    lt = wtb[:, b * 4096 + kc * 128:b * 4096 + (kc + 1) * 128]
                    xc = xns[:, kc * TOK:(kc + 1) * TOK]
                    st, sp = (kc == 0), (kc == DCH - 1)
                    t.matmul(pj[b][:, 0:512], lt, xc[:, 0:512], start=st, stop=sp)
                    mm = t.matmul(pj[b][:, 512:1024], lt, xc[:, 512:1024], start=st, stop=sp)
                mm.then_inc(TS_, 1)

        @block.vector
        def _(v):
            mean = stats[0:1, 0:TOK]; var = stats[0:1, TOK:2 * TOK]
            rstd = stats[0:1, 2 * TOK:3 * TOK]; mr = stats[0:1, 3 * TOK:4 * TOK]
            scr = stats[0:1, 4 * TOK:5 * TOK]; srt = stats[0:1, 5 * TOK:6 * TOK]
            v.wait_ge(TS_, DCH)
            v.tensor_scalar(mean, pm, 1.0 / D, None, ALU.mult)
            v.tensor_scalar(var, pv, 1.0 / D, None, ALU.mult)
            v.tensor_tensor(scr, mean, mean, ALU.mult)
            v.tensor_tensor(var, var, scr, ALU.subtract)
            v.tensor_scalar(scr, var, float(EPS_LN), None, ALU.add).then_inc(VS_, 1)
            v.wait_ge(AS_, a_sqrt)
            v.reciprocal(rstd, srt).then_inc(VS_, 1)
            v.wait_ge(VS_, 2)
            v.tensor_tensor(mr, mean, rstd, ALU.mult).then_inc(VS_, 1)
            v.wait_ge(TS_, t_bc1)
            v.tensor_copy(Rsb[:], pbc).then_inc(VS_, 1)
            v.wait_ge(TS_, t_bc2)
            v.tensor_copy(MRsb[:], pbc).then_inc(VS_, 1)
            for c in range(DCH):
                v.wait_ge(XF[c % 2], 16 * ((DCH + c) // 2 + 1))
                xc = xfc[:, (c % 2) * TOK:(c % 2 + 1) * TOK]
                xo = xns[:, c * TOK:(c + 1) * TOK]
                sqx = sq[:, (c % 2) * TOK:(c % 2 + 1) * TOK]
                v.tensor_tensor(sqx, xc, Rsb[:], ALU.mult)
                v.tensor_tensor(xo, sqx, MRsb[:], ALU.subtract).then_inc(VS_, 1)
                v.engine_nop().then_inc(XND, 1)
            for jt in range(NJT):
                b = jt % 2
                v.wait_ge(TS_, t_proj(jt))
                if jt >= 2:
                    v.wait_ge(OBS[jt % 2], 16 * ((jt - 2) // 2 + 1))
                if jt == NJT - 1:
                    v.tensor_copy(obf[:], pj[b])
                else:
                    v.tensor_copy(ob[:, b * TOK:(b + 1) * TOK], pj[b])
                v.engine_nop().then_inc(VS_, 1)

        @block.sync
        def _(s):
            for jt in range(NJT):
                b = jt % 2
                s.wait_ge(VS_, v_pe(jt))
                if jt < 64:
                    s.dma_start(out=gateo[jt * 128:(jt + 1) * 128, :],
                                in_=ob[:, b * TOK:(b + 1) * TOK]).then_inc(OBS[jt % 2], 16)
                elif jt < 144:
                    s.dma_start(out=xbcpo[(jt - 64) * 128:(jt - 63) * 128, :],
                                in_=ob[:, b * TOK:(b + 1) * TOK]).then_inc(OBS[jt % 2], 16)
                else:
                    s.dma_start(out=dto, in_=obf[:]).then_inc(OBS[jt % 2], 16)
            s.wait_ge(VS_, v_xnend)
            s.dma_start(out=xn2o.rearrange("(c p) t -> p c t", p=128),
                        in_=xns[:].rearrange("p (c t) -> p c t", c=DCH)).then_inc(SY_, 16)
            s.wait_ge(SY_, 16)
            s.wait_ge(OBS[0], 16 * 73)
            s.wait_ge(OBS[1], 16 * 72)
    es.close()
    return nc



ENGS = ("t", "v", "a", "g", "s")


class Phased:
    def __init__(self):
        self.phases = []

    def phase(self, *items):
        self.phases.append(list(items))

    def emit(self, nc, es, block):
        sems = {e: es.enter_context(nc.semaphore(f"ph_{e}")) for e in ENGS}
        cum = {e: [0] * (len(self.phases) + 1) for e in ENGS}
        for p, items in enumerate(self.phases):
            for e in ENGS:
                cum[e][p + 1] = cum[e][p] + sum(inc for (k, _, inc) in items if k == e)
        engs = {}

        def body(ekey, eng):
            for p, items in enumerate(self.phases):
                mine = [(k, f, inc) for (k, f, inc) in items if k == ekey]
                if not mine:
                    continue
                for o in ENGS:
                    if o != ekey and cum[o][p] > 0:
                        eng.wait_ge(sems[o], cum[o][p])
                if cum[ekey][p] > 0:
                    eng.wait_ge(sems[ekey], cum[ekey][p])
                for (_, f, inc) in mine:
                    ins = f(eng)
                    ins.then_inc(sems[ekey], inc)

        @block.tensor
        def _(t):
            body("t", t)

        @block.vector
        def _(v):
            body("v", v)

        @block.scalar
        def _(a):
            body("a", a)

        @block.gpsimd
        def _(g):
            body("g", g)

        @block.sync
        def _(s):
            body("s", s)
            for e in ENGS:
                if cum[e][len(self.phases)] > 0:
                    s.wait_ge(sems[e], cum[e][len(self.phases)])



def build_l1ba():
    nc = bass.Bass("TRN2", target_bir_lowering=False, debug=False)
    xbcp = nc.dram_tensor("xbcp", [10240, TOK], BF16, kind="ExternalInput").ap()
    halo = nc.dram_tensor("halo", [10240, 3], BF16, kind="ExternalInput").ap()
    cw = nc.dram_tensor("cw", [128, 320], F32, kind="ExternalInput").ap()
    cb = nc.dram_tensor("cb", [128, 80], F32, kind="ExternalInput").ap()
    dtr = nc.dram_tensor("dtr", [128, TOK], F32, kind="ExternalInput").ap()
    dtb = nc.dram_tensor("dtb", [128, 1], F32, kind="ExternalInput").ap()
    av = nc.dram_tensor("av", [128, 1], F32, kind="ExternalInput").ap()
    dexp = nc.dram_tensor("dexp", [128, 64], F32, kind="ExternalInput").ap()
    xbcs = nc.dram_tensor("xbcs", [10240, TOK], BF16, kind="ExternalOutput").ap()
    csec = nc.dram_tensor("csec", [1024, TOK], F32, kind="ExternalOutput").ap()
    xdo = nc.dram_tensor("xdo", [8192, TOK], BF16, kind="ExternalOutput").ap()
    cso = nc.dram_tensor("cso", [128, TOK], F32, kind="ExternalOutput").ap()
    wvo = nc.dram_tensor("wvo", [128, TOK], F32, kind="ExternalOutput").ap()
    wdo = nc.dram_tensor("wdo", [128, TOK], F32, kind="ExternalOutput").ap()
    cdco = nc.dram_tensor("cdco", [128, 8], F32, kind="ExternalOutput").ap()
    acoro = nc.dram_tensor("acoro", [128, 1], F32, kind="ExternalOutput").ap()

    NCC = 80
    es = contextlib.ExitStack()
    sb = lambda name, shape, dt=F32: es.enter_context(nc.sbuf_tensor(name, shape, dt))
    if True:
        consts = sb("consts", [128, 4])
        xpb = sb("xpb", [128, 2 * (TOK + 3)], BF16)
        cws = sb("cws", [128, NCC * 4])
        cbs = sb("cbs", [128, NCC])
        des = sb("des", [128, 64])
        acc = sb("acc", [128, 2 * TOK])
        sil = sb("sil", [128, 2 * TOK], BF16)
        silf = sb("silf", [128, TOK])
        xdb = sb("xdb", [128, 2 * TOK], BF16)
        dts = sb("dts", [128, 12 * TOK])
        cd8 = sb("cd8", [128, 16])
        rsneg = sb("rsneg", [128, 8])
        onesb = sb("onesb", [128, 128])
        block = es.enter_context(nc.Block())
        nc.const_aps.register(F32, 1.0, consts[:, 0:1])
        ph = Phased()
        P = ph.phase
        dsl = lambda i: dts[:, i * TOK:(i + 1) * TOK]
        dcol = lambda i: dts[:, 11 * TOK + 0:11 * TOK + 1] if False else None
        smal = sb("smal", [128, 4])

        P(("g", lambda g: g.memset(consts[:, 0:1], 1.0), 1),
          ("g", lambda g: g.dma_start(out=cws[:], in_=cw), 16),
          ("g", lambda g: g.dma_start(out=cbs[:], in_=cb), 16),
          ("g", lambda g: g.dma_start(out=des[:], in_=dexp), 16),
          ("g", lambda g: g.dma_start(out=dts[:, 0:TOK], in_=dtr), 16),
          ("g", lambda g: g.dma_start(out=smal[:, 0:1], in_=dtb), 16),
          ("g", lambda g: g.dma_start(out=smal[:, 1:2], in_=av), 16))
        P(("v", lambda v: v.tensor_scalar(dsl(1), dsl(0), smal[:, 0:1], None, ALU.add), 1))
        P(("a", lambda a: a.activation(dsl(2), dsl(1), AF.Abs), 1),
          ("v", lambda v: v.tensor_scalar(dsl(4), dsl(1), 0.0, None, ALU.max), 1))
        P(("a", lambda a: a.activation(dsl(3), dsl(2), AF.Exp, scale=-1.0), 1))
        P(("a", lambda a: a.activation(dsl(2), dsl(3), AF.Ln, bias=1.0), 1))
        P(("v", lambda v: v.tensor_tensor(dsl(5), dsl(4), dsl(2), ALU.add), 1))
        P(("v", lambda v: v.tensor_scalar(dsl(5), dsl(5), 0.001, 100.0, ALU.max, ALU.min), 1))
        P(("v", lambda v: v.tensor_scalar(dsl(6), dsl(5), smal[:, 1:2], None, ALU.mult), 1))
        P(("v", lambda v: v.tensor_copy(dsl(7), dsl(6)), 1))
        srcp = [7, 8]
        for k in [1, 2, 4, 8, 16, 32, 64]:
            items = []
            s_, d_ = srcp
            for c in range(8):
                s0 = c * 128
                items.append(("v", lambda v, s_=s_, d_=d_, s0=s0, k=k: v.tensor_tensor(
                    dsl(d_)[:, s0 + k:s0 + 128], dsl(s_)[:, s0 + k:s0 + 128],
                    dsl(s_)[:, s0:s0 + 128 - k], ALU.add), 1))
                items.append(("v", lambda v, s_=s_, d_=d_, s0=s0, k=k: v.tensor_copy(
                    dsl(d_)[:, s0:s0 + k], dsl(s_)[:, s0:s0 + k]), 1))
            P(*items)
            srcp = [srcp[1], srcp[0]]
        CS = srcp[0]
        P(*[("v", lambda v, c=c: v.tensor_copy(
                rsneg[:, c:c + 1], dsl(CS)[:, c * 128 + 127:c * 128 + 128]), 1)
            for c in range(8)])
        P(("v", lambda v: v.memset(onesb[:], 1.0), 1))
        items = []
        for c in range(8):
            items.append(("v", lambda v, c=c: v.tensor_scalar(
                dsl(7)[:, c * 128:(c + 1) * 128], onesb[:],
                rsneg[:, c:c + 1], None, ALU.mult), 1))
        P(*items)
        items = [("v", lambda v, c=c: v.tensor_tensor(
                     dsl(9)[:, c * 128:(c + 1) * 128], dsl(CS)[:, c * 128:(c + 1) * 128],
                     dsl(7)[:, c * 128:(c + 1) * 128], ALU.subtract), 1) for c in range(8)]
        items.append(("a", lambda a: a.activation(cd8[:, 0:8], rsneg[:], AF.Exp), 1))
        items.append(("v", lambda v: v.reduce_sum(cd8[:, 14:15], rsneg[:],
                                                  axis=mybir.AxisListType.X), 1))
        P(*items)
        P(("a", lambda a: a.activation(dsl(10), dsl(9), AF.Exp, scale=-1.0), 1))
        P(("v", lambda v: v.tensor_tensor(dsl(11), dsl(5), dsl(10), ALU.mult), 1),
          ("a", lambda a: a.activation(cd8[:, 15:16], cd8[:, 14:15], AF.Exp), 1))
        P(("s", lambda s: s.dma_start(out=cso, in_=dsl(CS)), 16),
          ("s", lambda s: s.dma_start(out=wvo, in_=dsl(5)), 16),
          ("s", lambda s: s.dma_start(out=wdo, in_=dsl(11)), 16),
          ("s", lambda s: s.dma_start(out=cdco, in_=cd8[:, 0:8]), 16),
          ("s", lambda s: s.dma_start(out=acoro, in_=cd8[:, 15:16]), 16))
        for cc in range(NCC):
            b = cc % 2
            xpc = xpb[:, b * (TOK + 3):(b + 1) * (TOK + 3)]
            a_ = acc[:, b * TOK:(b + 1) * TOK]
            sl = sil[:, b * TOK:(b + 1) * TOK]
            P(("g", lambda g, cc=cc, b=b: g.dma_start(
                  out=xpb[:, b * (TOK + 3):b * (TOK + 3) + 3],
                  in_=halo[cc * 128:(cc + 1) * 128, :]), 16),
              ("g", lambda g, cc=cc, b=b: g.dma_start(
                  out=xpb[:, b * (TOK + 3) + 3:(b + 1) * (TOK + 3)],
                  in_=xbcp[cc * 128:(cc + 1) * 128, :]), 16))
            P(("v", lambda v, xpc=xpc, a_=a_, cc=cc: v.tensor_scalar(
                  a_, xpc[:, 0:TOK], cws[:, 4 * cc:4 * cc + 1], None, ALU.mult), 1))
            for tau in range(1, 4):
                P(("v", lambda v, xpc=xpc, a_=a_, cc=cc, tau=tau: v.scalar_tensor_tensor(
                      a_, xpc[:, tau:tau + TOK], cws[:, 4 * cc + tau:4 * cc + tau + 1],
                      a_, ALU.mult, ALU.add), 1))
            P(("v", lambda v, a_=a_, cc=cc: v.tensor_scalar(
                  a_, a_, cbs[:, cc:cc + 1], None, ALU.add), 1))
            items = [("a", lambda a, sl=sl, a_=a_: a.activation(sl, a_, AF.Silu), 1)]
            if cc >= 72:
                items.append(("a", lambda a, a_=a_: a.activation(silf[:], a_, AF.Silu), 1))
            P(*items)
            st = [("s", lambda s, cc=cc, sl=sl: s.dma_start(
                out=xbcs[cc * 128:(cc + 1) * 128, :], in_=sl), 16)]
            if cc >= 72:
                st.append(("s", lambda s, cc=cc: s.dma_start(
                    out=csec[(cc - 72) * 128:(cc - 71) * 128, :], in_=silf[:]), 16))
            if cc < 64:
                st.append(("v", lambda v, cc=cc, b=b, sl=sl: v.tensor_scalar(
                    xdb[:, b * TOK:(b + 1) * TOK], sl, des[:, cc:cc + 1], None, ALU.mult), 1))
            P(*st)
            if cc < 64:
                P(("s", lambda s, cc=cc, b=b: s.dma_start(
                    out=xdo[cc * 128:(cc + 1) * 128, :],
                    in_=xdb[:, b * TOK:(b + 1) * TOK]), 16))
        ph.emit(nc, es, block)
    es.close()
    return nc



def build_l1bb():
    nc = bass.Bass("TRN2", target_bir_lowering=False, debug=False)
    xbcs = nc.dram_tensor("xbcs", [10240, TOK], BF16, kind="ExternalInput").ap()
    xtm = nc.dram_tensor("xtm", [TOK, 8192], BF16, kind="ExternalInput").ap()
    xdtm = nc.dram_tensor("xdtm", [TOK, 8192], BF16, kind="ExternalInput").ap()
    btm = nc.dram_tensor("btm", [TOK, 1024], BF16, kind="ExternalInput").ap()
    csr = nc.dram_tensor("csr", [128, TOK], F32, kind="ExternalInput").ap()
    wvt = nc.dram_tensor("wvt", [TOK, 128], F32, kind="ExternalInput").ap()
    wdt = nc.dram_tensor("wdt", [TOK, 128], F32, kind="ExternalInput").ap()
    cdc = nc.dram_tensor("cdc", [128, 1024], F32, kind="ExternalInput").ap()
    tri = nc.dram_tensor("tri", [128, 128], F32, kind="ExternalInput").ap()
    y1o = nc.dram_tensor("y1o", [TOK, 8192], BF16, kind="ExternalOutput").ap()
    sto = nc.dram_tensor("sto", [8, 128, 8192], BF16, kind="ExternalOutput").ap()
    bco = nc.dram_tensor("bco", [128, 8192], F32, kind="ExternalOutput").ap()

    es = contextlib.ExitStack()
    sb = lambda name, shape, dt=F32: es.enter_context(nc.sbuf_tensor(name, shape, dt))
    if True:
        bcfm = sb("bcfm", [128, 16 * 1024], BF16)
        btmc = sb("btmc", [128, 2 * 1024], BF16)
        xtmb = sb("xtmb", [128, 4096], BF16)
        xdtb = sb("xdtb", [128, 4096], BF16)
        wvts = sb("wvts", [128, 1024])
        wdts = sb("wdts", [128, 1024])
        csrq = sb("csrq", [1, 4096])
        l33 = sb("l33", [33, 32 * 128])
        r33 = sb("r33", [33, 32 * 128])
        lseg = sb("lseg", [128, 32 * 128], BF16)
        mstT = sb("mstT", [128, 512], BF16)
        stT = sb("stT", [128, 32 * 128], BF16)
        xv = sb("xv", [128, 2048], BF16)
        xw = sb("xw", [128, 2048], BF16)
        y1b = sb("y1b", [128, 4096], BF16)
        stb = sb("stb", [128, 4096], BF16)
        bcore = sb("bcore", [128, 8192])
        cdcs = sb("cdcs", [128, 1024])
        tris = sb("tris", [128, 128], BF16)
        pool = es.enter_context(nc.psum_tensor("pool", [128, 4096], F32))
        block = es.enter_context(nc.Block())
        ph = Phased()
        P = ph.phase

        P(("g", lambda g: g.memset(bcore[:], 0.0), 1),
          ("g", lambda g: g.memset(l33[:], 0.0), 1),
          ("g", lambda g: g.memset(r33[:], 0.0), 1),
          ("g", lambda g: g.dma_start(out=bcfm[:].rearrange("p (c t) -> p c t", c=16),
                                      in_=xbcs[8192:10240, :].rearrange("(c p) t -> p c t", p=128)), 16),
          ("g", lambda g: g.dma_start(out=wvts[:].rearrange("p (c h) -> p c h", c=8),
                                      in_=wvt.rearrange("(c p) h -> p c h", p=128)), 16),
          ("g", lambda g: g.dma_start(out=wdts[:].rearrange("p (c h) -> p c h", c=8),
                                      in_=wdt.rearrange("(c p) h -> p c h", p=128)), 16),
          ("g", lambda g: g.dma_start(out=cdcs[:], in_=cdc), 16),
          ("g", lambda g: g.dma_start(out=tris[:], in_=tri), 16))
        P(("v", lambda v: v.memset(l33[0:1, :], 1.0), 1),
          ("a", lambda a: a.memzero(r33[32:33, :]), 1))
        P(("a", lambda a: a.activation(r33[32:33, :], r33[32:33, :], AF.Exp), 1))
        for c in range(8):
            cb_ = c % 2
            for hf in range(2):
                b = (2 * c + hf) % 2
                xh = xtmb[:]
                xdh = xdtb[:]
                y1h = y1b[:]
                sth = stb[:]
                items = [
                    ("g", lambda g, c=c, hf=hf, xh=xh: g.dma_start(
                        out=xh, in_=xtm[c * 128:(c + 1) * 128, hf * 4096:(hf + 1) * 4096]), 16),
                    ("g", lambda g, c=c, hf=hf, xdh=xdh: g.dma_start(
                        out=xdh, in_=xdtm[c * 128:(c + 1) * 128, hf * 4096:(hf + 1) * 4096]), 16)]
                if hf == 0:
                    items.append(("g", lambda g, c=c, cb_=cb_: g.dma_start(
                        out=btmc[:, cb_ * 1024:(cb_ + 1) * 1024],
                        in_=btm[c * 128:(c + 1) * 128, :]), 16))
                P(*items)
                P(*[("t", lambda t, c=c, hf=hf, g4=g4: t.matmul(
                        pool[:, g4 * 128:(g4 + 1) * 128],
                        bcfm[:, (4 * hf + g4) * 1024 + c * 128:(4 * hf + g4) * 1024 + (c + 1) * 128],
                        bcfm[:, (8 + 4 * hf + g4) * 1024 + c * 128:(8 + 4 * hf + g4) * 1024 + (c + 1) * 128],
                        start=True, stop=True), 1) for g4 in range(4)])
                P(*[("v", lambda v, g4=g4: v.tensor_tensor(
                        mstT[:, g4 * 128:(g4 + 1) * 128], pool[:, g4 * 128:(g4 + 1) * 128],
                        tris[:], ALU.mult), 1) for g4 in range(4)])
                for q in range(2):
                    hq = hf * 64 + q * 32
                    items = [("g", lambda g, c=c, hq=hq: g.dma_start(
                        out=csrq[:].rearrange("o (h l) -> o h l", h=32),
                        in_=csr[hq:hq + 32, c * 128:(c + 1) * 128].unsqueeze(0)), 16)]
                    for i in range(32):
                        h = hq + i
                        hloc = q * 32 + i
                        items.append(("v", lambda v, i=i, h=h, c=c, xh=xh, hloc=hloc: v.tensor_scalar(
                            xv[:, i * 64:(i + 1) * 64], xh[:, hloc * 64:(hloc + 1) * 64],
                            wvts[:, c * 128 + h:c * 128 + h + 1], None, ALU.mult), 1))
                        items.append(("v", lambda v, i=i, h=h, c=c, xh=xh, hloc=hloc: v.tensor_scalar(
                            xw[:, i * 64:(i + 1) * 64], xh[:, hloc * 64:(hloc + 1) * 64],
                            wdts[:, c * 128 + h:c * 128 + h + 1], None, ALU.mult), 1))
                    P(*items)
                    items = []
                    for i in range(32):
                        items.append(("v", lambda v, i=i: v.tensor_scalar(
                            l33[32:33, i * 128:(i + 1) * 128],
                            csrq[0:1, i * 128:(i + 1) * 128], -1.0, None, ALU.mult), 1))
                        items.append(("v", lambda v, i=i: v.tensor_copy(
                            r33[0:1, i * 128:(i + 1) * 128],
                            csrq[0:1, i * 128:(i + 1) * 128]), 1))
                    P(*items)
                    P(*[("t", lambda t, i=i: t.matmul(
                            pool[:, i * 128:(i + 1) * 128], l33[:, i * 128:(i + 1) * 128],
                            r33[:, i * 128:(i + 1) * 128], start=True, stop=True), 1)
                        for i in range(32)])
                    P(*[("v", lambda v, i=i: v.tensor_scalar(
                            pool[:, i * 128:(i + 1) * 128], pool[:, i * 128:(i + 1) * 128],
                            0.0, None, ALU.min), 1) for i in range(32)])
                    P(*[("a", lambda a, i=i: a.activation(
                            lseg[:, i * 128:(i + 1) * 128], pool[:, i * 128:(i + 1) * 128],
                            AF.Exp), 1) for i in range(32)])
                    P(*[("v", lambda v, i=i, q=q: v.tensor_tensor(
                            stT[:, i * 128:(i + 1) * 128], lseg[:, i * 128:(i + 1) * 128],
                            mstT[:, (q * 32 + i) // 16 * 128:((q * 32 + i) // 16 + 1) * 128],
                            ALU.mult), 1) for i in range(32)])
                    P(*[("t", lambda t, i=i, q=q, hl=q * 32: t.matmul(
                            pool[:, q * 2048 + i * 64:q * 2048 + (i + 1) * 64],
                            stT[:, i * 128:(i + 1) * 128],
                            xv[:, i * 64:(i + 1) * 64],
                            start=True, stop=True), 1) for i in range(32)])
                    P(*[("v", lambda v, i=i, q=q, y1h=y1h, xdh=xdh: v.tensor_tensor(
                            y1h[:, (q * 32 + i) * 64:(q * 32 + i + 1) * 64],
                            pool[:, q * 2048 + i * 64:q * 2048 + (i + 1) * 64],
                            xdh[:, (q * 32 + i) * 64:(q * 32 + i + 1) * 64], ALU.add), 1)
                        for i in range(32)])
                    P(*[("t", lambda t, i=i, q=q, hf=hf, cb_=cb_: t.matmul(
                            pool[:, q * 2048 + i * 64:q * 2048 + (i + 1) * 64],
                            btmc[:, cb_ * 1024 + (4 * hf + (q * 32 + i) // 16) * 128:
                                 cb_ * 1024 + (4 * hf + (q * 32 + i) // 16 + 1) * 128],
                            xw[:, i * 64:(i + 1) * 64],
                            start=True, stop=True), 1) for i in range(32)])
                    P(*[("v", lambda v, i=i, q=q, sth=sth: v.tensor_copy(
                            sth[:, (q * 32 + i) * 64:(q * 32 + i + 1) * 64],
                            pool[:, q * 2048 + i * 64:q * 2048 + (i + 1) * 64]), 1)
                        for i in range(32)])
                items = [("v", lambda v, hl=hl, h=hf * 64 + hl, c=c, sth=sth: v.scalar_tensor_tensor(
                            bcore[:, h * 64:(h + 1) * 64], bcore[:, h * 64:(h + 1) * 64],
                            cdcs[:, c * 128 + h:c * 128 + h + 1], sth[:, hl * 64:(hl + 1) * 64],
                            ALU.mult, ALU.add), 1) for hl in range(64)]
                items.append(("s", lambda s, c=c, hf=hf, y1h=y1h: s.dma_start(
                    out=y1o[c * 128:(c + 1) * 128, hf * 4096:(hf + 1) * 4096], in_=y1h), 16))
                items.append(("s", lambda s, c=c, hf=hf, sth=sth: s.dma_start(
                    out=sto[c][:, hf * 4096:(hf + 1) * 4096], in_=sth), 16))
                P(*items)
        P(("s", lambda s: s.dma_start(out=bco, in_=bcore[:]), 16))
        ph.emit(nc, es, block)
    es.close()
    return nc



def build_l1c():
    nc = bass.Bass("TRN2", target_bir_lowering=False, debug=False)
    y1 = nc.dram_tensor("y1", [TOK, 8192], BF16, kind="ExternalInput").ap()
    csec = nc.dram_tensor("csec", [1024, TOK], F32, kind="ExternalInput").ap()
    states = nc.dram_tensor("states", [8, 128, 8192], BF16, kind="ExternalInput").ap()
    bpre = nc.dram_tensor("bpre", [7, 128, 8192], F32, kind="ExternalInput").ap()
    aval = nc.dram_tensor("aval", [128, 7 * 128], F32, kind="ExternalInput").ap()
    cdc = nc.dram_tensor("cdc", [128, 1024], F32, kind="ExternalInput").ap()
    cst = nc.dram_tensor("cst", [TOK, 128], F32, kind="ExternalInput").ap()
    gatet = nc.dram_tensor("gatet", [TOK, 8192], BF16, kind="ExternalInput").ap()
    nwb = nc.dram_tensor("nwb", [128, 8192], BF16, kind="ExternalInput").ap()
    ygno = nc.dram_tensor("ygno", [TOK, 8192], BF16, kind="ExternalOutput").ap()

    EPS_RMS = 1e-5
    es = contextlib.ExitStack()
    sb = lambda name, shape, dt=F32: es.enter_context(nc.sbuf_tensor(name, shape, dt))
    if True:
        csecs = sb("csecs", [128, 8 * 1024])
        carry = sb("carry", [128, 8192])
        y1c = sb("y1c", [128, 8192], BF16)
        gatec = sb("gatec", [128, 8192], BF16)
        statc = sb("statc", [128, 8192], BF16)
        yfull = sb("yfull", [128, 8192], BF16)
        silg = sb("silg", [128, 8192], BF16)
        sq = sb("sq", [128, 8192], BF16)
        scrF = sb("scrF", [128, 4096])
        csts = sb("csts", [128, 1024])
        cdcs = sb("cdcs", [128, 1024])
        nwbs = sb("nwbs", [128, 8192], BF16)
        avs = sb("avs", [128, 7 * 128])
        uTb = sb("uTb", [128, 128])
        rs = sb("rs", [128, 32])
        rsb_ = sb("rsb_", [128, 8])
        rsc_ = sb("rsc_", [128, 8])
        epst = sb("epst", [128, 1])
        pool = es.enter_context(nc.psum_tensor("pool", [128, 4096], F32))
        block = es.enter_context(nc.Block())
        nc.const_aps.register(F32, float(EPS_RMS), epst[:, 0:1])
        ph = Phased()
        P = ph.phase

        P(("g", lambda g: g.memset(carry[:], 0.0), 1),
          ("g", lambda g: g.memset(epst[:, 0:1], EPS_RMS), 1),
          ("g", lambda g: g.dma_start(out=csecs[:].rearrange("p (c t) -> p c t", c=8),
                                      in_=csec.rearrange("(c p) t -> p c t", p=128)), 16),
          ("g", lambda g: g.dma_start(out=csts[:].rearrange("p (c h) -> p c h", c=8),
                                      in_=cst.rearrange("(c p) h -> p c h", p=128)), 16),
          ("g", lambda g: g.dma_start(out=cdcs[:], in_=cdc), 16),
          ("g", lambda g: g.dma_start(out=nwbs[:], in_=nwb), 16),
          ("g", lambda g: g.dma_start(out=avs[:], in_=aval), 16))
        for s_ in range(7):
            for hh in range(2):
                P(("g", lambda g, s_=s_, hh=hh: g.dma_start(
                    out=scrF[:], in_=bpre[s_][:, hh * 4096:(hh + 1) * 4096]), 16))
                P(*[("v", lambda v, s_=s_, hh=hh, hl=hl: v.scalar_tensor_tensor(
                        carry[:, hh * 4096 + hl * 64:hh * 4096 + (hl + 1) * 64],
                        carry[:, hh * 4096 + hl * 64:hh * 4096 + (hl + 1) * 64],
                        avs[:, s_ * 128 + hh * 64 + hl:s_ * 128 + hh * 64 + hl + 1],
                        scrF[:, hl * 64:(hl + 1) * 64], ALU.mult, ALU.add), 1)
                    for hl in range(64)])
        for c in range(8):
            P(("g", lambda g, c=c: g.dma_start(out=y1c[:], in_=y1[c * 128:(c + 1) * 128, :]), 16),
              ("g", lambda g, c=c: g.dma_start(out=gatec[:], in_=gatet[c * 128:(c + 1) * 128, :]), 16),
              ("g", lambda g, c=c: g.dma_start(out=statc[:], in_=states[c]), 16),
              ("a", lambda a, c=c: a.activation(uTb[:], csts[:, c * 128:(c + 1) * 128], AF.Exp), 1))
            for hf in range(2):
                P(*[("t", lambda t, c=c, hl=hl, h=hf * 64 + hl: t.matmul(
                        pool[:, hl * 64:(hl + 1) * 64],
                        csecs[:, (h // 16) * 1024 + c * 128:(h // 16) * 1024 + (c + 1) * 128],
                        carry[:, h * 64:(h + 1) * 64], start=True, stop=True), 1)
                    for hl in range(64)])
                P(*[("v", lambda v, hl=hl, h=hf * 64 + hl: v.scalar_tensor_tensor(
                        yfull[:, h * 64:(h + 1) * 64], pool[:, hl * 64:(hl + 1) * 64],
                        uTb[:, h:h + 1], y1c[:, h * 64:(h + 1) * 64], ALU.mult, ALU.add), 1)
                    for hl in range(64)])
            P(*[("v", lambda v, c=c, h=h: v.scalar_tensor_tensor(
                    carry[:, h * 64:(h + 1) * 64], carry[:, h * 64:(h + 1) * 64],
                    cdcs[:, c * 128 + h:c * 128 + h + 1], statc[:, h * 64:(h + 1) * 64],
                    ALU.mult, ALU.add), 1) for h in range(128)])
            P(("a", lambda a: a.activation(silg[:], gatec[:], AF.Silu), 1))
            P(("v", lambda v: v.tensor_tensor(yfull[:], yfull[:], silg[:], ALU.mult), 1))
            P(("a", lambda a: a.activation(sq[:], yfull[:], AF.Square), 1))
            P(*[("v", lambda v, j=j: v.reduce_sum(rs[:, j:j + 1], sq[:, j * 1024:(j + 1) * 1024],
                                                  axis=mybir.AxisListType.X), 1) for j in range(8)])
            P(("v", lambda v: v.tensor_scalar(rsb_[:], rs[:, 0:8], 1.0 / 1024.0, float(EPS_RMS),
                                              ALU.mult, ALU.add), 1))
            P(("a", lambda a: a.activation(rsc_[:], rsb_[:], AF.Ln), 1))
            P(("a", lambda a: a.activation(rsb_[:], rsc_[:], AF.Exp, scale=-0.5), 1))
            P(*[("v", lambda v, j=j: v.tensor_scalar(yfull[:, j * 1024:(j + 1) * 1024],
                                                     yfull[:, j * 1024:(j + 1) * 1024],
                                                     rsb_[:, j:j + 1], None, ALU.mult), 1)
                for j in range(8)])
            P(("v", lambda v: v.tensor_tensor(yfull[:], yfull[:], nwbs[:], ALU.mult), 1))
            P(("s", lambda s, c=c: s.dma_start(out=ygno[c * 128:(c + 1) * 128, :], in_=yfull[:]), 16))
        ph.emit(nc, es, block)
    es.close()
    return nc



def build_l1c2():
    nc = bass.Bass("TRN2", target_bir_lowering=False, debug=False)
    ygn = nc.dram_tensor("ygn", [8192, TOK], BF16, kind="ExternalInput").ap()
    wopT = nc.dram_tensor("wopT", [8192, D], F32, kind="ExternalInput").ap()
    xpo = nc.dram_tensor("xpo", [D, TOK], F32, kind="ExternalOutput").ap()

    es = contextlib.ExitStack()
    sb = lambda name, shape, dt=F32: es.enter_context(nc.sbuf_tensor(name, shape, dt))
    if True:
        ygns = sb("ygns", [128, 64 * TOK], BF16)
        wtb = sb("wtb", [128, 2 * 64 * 128], BF16)
        xcb = sb("xcb", [128, 1024])
        ob = sb("ob", [128, 2 * TOK])
        pool = es.enter_context(nc.psum_tensor("pool", [128, 4096], F32))
        block = es.enter_context(nc.Block())
        ph = Phased()
        P = ph.phase
        P(("g", lambda g: g.dma_start(out=ygns[:].rearrange("p (c t) -> p c t", c=64),
                                      in_=ygn.rearrange("(c p) t -> p c t", p=128)), 16))
        for jt in range(32):
            b = jt % 2
            P(("g", lambda g, jt=jt, b=b: g.dma_start(
                  out=wtb[:, b * 8192:(b + 1) * 8192].rearrange("p (c j) -> p c j", c=64),
                  in_=wopT[:, jt * 128:(jt + 1) * 128].rearrange("(c p) j -> p c j", p=128)), 16),
              )
            items = []
            for kc in range(64):
                items.append(("t", lambda t, b=b, kc=kc: t.matmul(
                    pool[:, (0 if b == 0 else 2048) + 0:(0 if b == 0 else 2048) + 512],
                    wtb[:, b * 8192 + kc * 128:b * 8192 + (kc + 1) * 128],
                    ygns[:, kc * TOK:kc * TOK + 512], start=(kc == 0), stop=(kc == 63)), 1))
                items.append(("t", lambda t, b=b, kc=kc: t.matmul(
                    pool[:, (0 if b == 0 else 2048) + 512:(0 if b == 0 else 2048) + 1024],
                    wtb[:, b * 8192 + kc * 128:b * 8192 + (kc + 1) * 128],
                    ygns[:, kc * TOK + 512:(kc + 1) * TOK], start=(kc == 0), stop=(kc == 63)), 1))
            P(*items)
            P(("v", lambda v, b=b: v.tensor_copy(
                ob[:, b * TOK:(b + 1) * TOK],
                pool[:, (0 if b == 0 else 2048):(0 if b == 0 else 2048) + 1024]), 1))
            P(("s", lambda s, jt=jt, b=b: s.dma_start(
                out=xpo[jt * 128:(jt + 1) * 128, :], in_=ob[:, b * TOK:(b + 1) * TOK]), 16))
        ph.emit(nc, es, block)
    es.close()
    return nc



def kernel(**inp):
    import ml_dtypes
    BF = ml_dtypes.bfloat16

    def run(nc, maps):
        res = run_bass_kernel_spmd(nc, maps, core_ids=list(range(NCORES)))
        return res.results

    rs = run(build_prep(), [stage_prep_inputs(inp, c) for c in range(NCORES)])
    gu_sl = [r['gu_w'] for r in rs]; dn_sl = [r['dn_w'] for r in rs]
    gate_sl = [r['gate_w'] for r in rs]; up_sl = [r['up_w'] for r in rs]
    down_sl = [r['down_w'] for r in rs]

    x = inp['L0_attn_x']
    rs = run(build_l0a(), stage_l0a_inputs(x, inp['L0_attn_Wq_d'], inp['L0_attn_bq_d'],
                                           inp['L0_attn_Wkv_d'], inp['L0_attn_bkv_d']))
    xn_l = [r['xn'] for r in rs]
    cq_all = np.concatenate([r['cqT'].T for r in rs], axis=0)
    ckv_all = np.concatenate([r['ckvT'].T for r in rs], axis=0)

    rs = run(build_l0b(), stage_l0b_inputs(cq_all, ckv_all, inp['L0_attn_W_qk'],
                                           inp['L0_attn_b_qk'], inp['L0_attn_Wv_u'],
                                           inp['L0_attn_bv_u'], inp['L0_attn_Wo'],
                                           inp['L0_attn_bo']))
    aout_l = [r['aout'] for r in rs]

    gu_blk, dn_blk = stage_l0c_inputs_weights(gu_sl, dn_sl)
    xT = np.ascontiguousarray(x.reshape(S_FULL, D).T)
    maps = [{"xT": np.ascontiguousarray(xT[:, c * TOK:(c + 1) * TOK]), "aout": aout_l[c],
             "xn": xn_l[c], "gu": gu_blk, "dnw": dn_blk} for c in range(NCORES)]
    rs = run(build_l0c(), maps)
    x1_l = [r['x1'] for r in rs]

    wpT = np.ascontiguousarray(inp['L1_attn_in_proj_w'].T)
    maps = [{"x1T": x1_l[c], "wpT": wpT} for c in range(NCORES)]
    rs = run(build_l1a(), maps)
    xn2_l = [r['xn2'] for r in rs]; gate_l = [r['gate'] for r in rs]
    xbcp_l = [r['xbcp'] for r in rs]; dtr_l = [r['dtr'] for r in rs]

    D_ = inp['L1_attn_D']
    cw = np.ascontiguousarray(inp['L1_attn_conv_w'].reshape(10240, 4)
                              .reshape(80, 128, 4).transpose(1, 0, 2).reshape(128, 320))
    cb = np.ascontiguousarray(inp['L1_attn_conv_b'].reshape(80, 128).T)
    dtb = inp['L1_attn_dt_bias'].reshape(128, 1)
    av = -(np.arange(128, dtype=np.float32) + 1).reshape(128, 1)
    cc_, p_ = np.meshgrid(np.arange(64), np.arange(128), indexing='xy')
    dexp = np.ascontiguousarray(D_[(128 * cc_ + p_) // 64].astype(np.float32))
    maps = []
    for c in range(NCORES):
        halo = (np.zeros((10240, 3), BF) if c == 0 else xbcp_l[c - 1][:, -3:])
        maps.append({"xbcp": xbcp_l[c], "halo": np.ascontiguousarray(halo), "cw": cw,
                     "cb": cb, "dtr": dtr_l[c], "dtb": dtb, "av": av, "dexp": dexp})
    rs = run(build_l1ba(), maps)
    xbcs_l = [r['xbcs'] for r in rs]; csec_l = [r['csec'] for r in rs]
    xdo_l = [r['xdo'] for r in rs]; cso_l = [r['cso'] for r in rs]
    wvo_l = [r['wvo'] for r in rs]; wdo_l = [r['wdo'] for r in rs]
    cdco_l = [r['cdco'] for r in rs]; acoro_l = [r['acoro'] for r in rs]

    tri = np.triu(np.ones((128, 128), np.float32))
    maps = []
    cdc_l = []
    for c in range(NCORES):
        cdc = np.ascontiguousarray(np.broadcast_to(cdco_l[c].T.reshape(1, 8, 128),
                                                   (128, 8, 128)).reshape(128, 1024))
        cdc_l.append(cdc)
        maps.append({"xbcs": xbcs_l[c],
                     "xtm": np.ascontiguousarray(xbcs_l[c][0:8192].T),
                     "xdtm": np.ascontiguousarray(xdo_l[c].T),
                     "btm": np.ascontiguousarray(xbcs_l[c][8192:9216].T),
                     "csr": cso_l[c],
                     "wvt": np.ascontiguousarray(wvo_l[c].T),
                     "wdt": np.ascontiguousarray(wdo_l[c].T),
                     "cdc": cdc, "tri": tri})
    rs = run(build_l1bb(), maps)
    y1_l = [r['y1o'] for r in rs]; sto_l = [r['sto'] for r in rs]; bco_l = [r['bco'] for r in rs]

    nwb = np.ascontiguousarray(np.broadcast_to(inp['L1_attn_norm_w'].astype(BF), (128, 8192)))
    maps = []
    for c in range(NCORES):
        bpre = np.zeros((7, 128, 8192), np.float32)
        aval = np.ones((128, 7, 128), np.float32)
        for j in range(c):
            s_ = 7 - c + j
            bpre[s_] = bco_l[j]
            if j >= 1:
                aval[:, s_, :] = acoro_l[j][:, 0][None, :]
        maps.append({"y1": y1_l[c], "csec": csec_l[c], "states": sto_l[c], "bpre": bpre,
                     "aval": np.ascontiguousarray(aval.reshape(128, 7 * 128)),
                     "cdc": cdc_l[c], "cst": np.ascontiguousarray(cso_l[c].T),
                     "gatet": np.ascontiguousarray(gate_l[c].T), "nwb": nwb})
    rs = run(build_l1c(), maps)
    ygn_l = [r['ygno'] for r in rs]

    wopT = np.ascontiguousarray(inp['L1_attn_out_proj_w'].T)
    maps = [{"ygn": np.ascontiguousarray(ygn_l[c].T), "wopT": wopT} for c in range(NCORES)]
    rs = run(build_l1c2(), maps)
    xpo_l = [r['xpo'] for r in rs]

    gate_full = np.concatenate(gate_sl, axis=1)
    up_full = np.concatenate(up_sl, axis=1)
    down_full = np.concatenate(down_sl, axis=1)
    gu_blk2 = np.empty((NIT, D, 256), BF)
    for it in range(NIT):
        gu_blk2[it, :, 0:128] = gate_full[:, it * 128:(it + 1) * 128]
        gu_blk2[it, :, 128:256] = up_full[:, it * 128:(it + 1) * 128]
    dn_blk2 = np.ascontiguousarray(down_full.reshape(11008, 4, 1024).transpose(1, 0, 2))
    gu_blk2 = np.ascontiguousarray(gu_blk2)
    maps = [{"xT": x1_l[c], "aout": xpo_l[c], "xn": xn2_l[c],
             "gu": gu_blk2, "dnw": dn_blk2} for c in range(NCORES)]
    rs = run(build_l0c(), maps)

    out = np.concatenate([r['x1'].T for r in rs], axis=0)
    return np.ascontiguousarray(out.reshape(1, S_FULL, D)).astype(np.float32)
